# version 01 (final)
# baseline (speedup 1.0000x reference)
import math
import numpy as np
from contextlib import ExitStack

import concourse.bass as bass
import concourse.mybir as mybir
from concourse.bass_utils import run_bass_kernel_spmd

F32 = mybir.dt.float32
BF16 = mybir.dt.bfloat16
AF = mybir.ActivationFunctionType
ALU = mybir.AluOpType
AX = mybir.AxisListType.X

B, S, HID = 2, 4096, 2048
NCORE = 8
T = 1024
TH = 512
HC = 16
E, I, IS = 16, 512, 2048
NH, D, RD, KVC, QC = 16, 128, 64, 512, 1536
HALF = 32
SCALE = float(1.0 / np.sqrt(np.float32(128.0)))
EPS = 1e-5
ROW_B = 32

_STAGE = 3
_FFN_CUT = "full"



def _prep_shared(inp):
    f32 = np.float32
    cc = np.ascontiguousarray
    d = {}

    w1T = np.asarray(inp["L0_ffn_w1"], f32).transpose(0, 2, 1)
    w3T = np.asarray(inp["L0_ffn_w3"], f32).transpose(0, 2, 1)
    pieces = np.empty((E, 4, HID, 256), f32)
    pieces[:, 0] = w1T[:, :, 0:256]
    pieces[:, 1] = w3T[:, :, 0:256]
    pieces[:, 2] = w1T[:, :, 256:512]
    pieces[:, 3] = w3T[:, :, 256:512]
    d["w13"] = cc(pieces)
    d["w2T"] = cc(np.asarray(inp["L0_ffn_w2"], f32).transpose(0, 2, 1))

    wsgT = np.asarray(inp["L0_ffn_ws_gate"], f32).T
    wsuT = np.asarray(inp["L0_ffn_ws_up"], f32).T
    gu = np.empty((16, HID, 256), f32)
    for j in range(8):
        gu[2 * j] = wsgT[:, j * 256:(j + 1) * 256]
        gu[2 * j + 1] = wsuT[:, j * 256:(j + 1) * 256]
    d["wsgu"] = cc(gu)
    wsdT = np.asarray(inp["L0_ffn_ws_down"], f32).T
    d["wsd"] = cc(wsdT.reshape(IS, 8, 256).transpose(1, 0, 2))

    d["gate_wT"] = cc(np.asarray(inp["L0_ffn_gate_w"], f32).T
                      .reshape(16, 128, E).transpose(1, 0, 2).reshape(128, 16 * E))
    d["sgwT"] = cc(np.asarray(inp["L0_ffn_shared_gate_w"], f32).T
                   .reshape(16, 128).T)

    d["wdkv"] = cc(np.asarray(inp["L0_attn_W_DKV"], f32))
    d["wdq"] = cc(np.asarray(inp["L0_attn_W_DQ"], f32))
    d["wuq"] = cc(np.asarray(inp["L0_attn_W_UQ"], f32))
    d["wuk"] = cc(np.asarray(inp["L0_attn_W_UK"], f32))
    d["wuv"] = cc(np.asarray(inp["L0_attn_W_UV"], f32))
    d["wqr"] = cc(np.asarray(inp["L0_attn_W_QR"], f32))
    d["wkr"] = cc(np.asarray(inp["L0_attn_W_KR"], f32))
    d["wo"] = cc(np.asarray(inp["L0_attn_W_O"], f32))

    def chunked(b, n):
        return cc(np.asarray(b, f32).reshape(n, 128).T)
    d["bdkv"] = chunked(inp["L0_attn_b_DKV"], 4)
    d["bdq"] = chunked(inp["L0_attn_b_DQ"], 12)
    d["buq"] = chunked(inp["L0_attn_b_UQ"], 16)
    d["buk"] = chunked(inp["L0_attn_b_UK"], 16)
    d["bo"] = chunked(inp["L0_attn_b_O"], 16)
    d["buv"] = chunked(inp["L0_attn_b_UV"], 16)
    d["ident"] = np.eye(128, dtype=f32)
    return d


def _prep_core(inp, core):
    b, part = core // 4, core % 4
    r0 = part * T
    f32 = np.float32
    d = {}
    x = np.asarray(inp["L0_attn_hidden_states"], f32)
    d["x_fm"] = np.ascontiguousarray(x[b, r0:r0 + T, :].T)

    j = np.arange(0, RD, 2, dtype=f32) / f32(RD)
    inv_freq = (1.0 / (f32(10000.0) ** j)).astype(f32)
    t = np.arange(r0, r0 + T, dtype=f32)
    freqs = (t[None, :] * inv_freq[:, None]).astype(f32)
    d["cosq"] = np.cos(freqs.astype(np.float64)).astype(f32)
    d["sinq"] = np.sin(freqs.astype(np.float64)).astype(f32)
    return d



class Sem:
    def __init__(self, ctx, nc, name, step=1):
        self.h = ctx.enter_context(nc.semaphore(name))
        self.v = 0
        self.step = step

    def bump(self):
        self.v += self.step
        return self.v


class P:

    ENGS = ("gpsimd", "sync", "tensor", "vector", "scalar")

    def __init__(self, nc, ctx):
        self.nc = nc
        self.q = {e: [] for e in self.ENGS}
        self.ps = [ctx.enter_context(nc.psum_tensor(f"ps{i}", [128, TH], F32))
                   for i in range(8)]
        self.bank_guard = [None] * 8

    def on(self, eng, fn):
        self.q[eng].append(fn)

    def flush_block(self):
        with self.nc.Block() as block:
            for name in self.ENGS:
                fns = self.q[name]
                if fns:
                    def body(eng, fns=fns):
                        for f in fns:
                            f(eng)
                    getattr(block, name)(body)
        self.q = {e: [] for e in self.ENGS}

    def mm(self, bank, out_ap, lhs, rhs, start, stop, inc=None, wait=None):
        w = list(wait or [])
        if start and bank is not None and self.bank_guard[bank] is not None:
            w.append(self.bank_guard[bank])
            self.bank_guard[bank] = None
        if inc is not None:
            inc.v += inc.step

        def f(t, out_ap=out_ap, lhs=lhs, rhs=rhs, start=start, stop=stop,
              inc=inc, w=w):
            for sem, val in w:
                t.wait_ge(sem.h, val)
            ins = t.matmul(out_ap, lhs, rhs, start=start, stop=stop)
            if inc is not None:
                ins.then_inc(inc.h, inc.step)
        self.on("tensor", f)

    def transpose(self, bank, out_ap, in_ap, ident_ap, inc=None, wait=None):
        w = list(wait or [])
        if bank is not None and self.bank_guard[bank] is not None:
            w.append(self.bank_guard[bank])
            self.bank_guard[bank] = None
        if inc is not None:
            inc.v += inc.step

        def f(t, out_ap=out_ap, in_ap=in_ap, ident_ap=ident_ap, inc=inc, w=w):
            for sem, val in w:
                t.wait_ge(sem.h, val)
            ins = t.transpose(out_ap, in_ap, ident_ap)
            if inc is not None:
                ins.then_inc(inc.h, inc.step)
        self.on("tensor", f)

    def guard(self, bank, sem, val):
        self.bank_guard[bank] = (sem, val)


def build_program():
    nc = bass.Bass("TRN2", target_bir_lowering=False, dynamic_dma_scratch_size=16384)
    dt = nc.dram_tensor

    x_in = dt("x_fm", [HID, T], F32, kind="ExternalInput")
    y_out = dt("y", [HID, T], F32, kind="ExternalOutput")
    cosq_in = dt("cosq", [HALF, T], F32, kind="ExternalInput")
    sinq_in = dt("sinq", [HALF, T], F32, kind="ExternalInput")
    ident_in = dt("ident", [128, 128], F32, kind="ExternalInput")
    w13_in = dt("w13", [E, 4, HID, 256], F32, kind="ExternalInput")
    w2T_in = dt("w2T", [E, I, HID], F32, kind="ExternalInput")
    wsgu_in = dt("wsgu", [16, HID, 256], F32, kind="ExternalInput")
    wsd_in = dt("wsd", [8, IS, 256], F32, kind="ExternalInput")
    gate_wT_in = dt("gate_wT", [128, 16 * E], F32, kind="ExternalInput")
    sgwT_in = dt("sgwT", [128, 16], F32, kind="ExternalInput")
    wdkv_in = dt("wdkv", [HID, KVC], F32, kind="ExternalInput")
    wdq_in = dt("wdq", [HID, QC], F32, kind="ExternalInput")
    wuq_in = dt("wuq", [QC, HID], F32, kind="ExternalInput")
    wuk_in = dt("wuk", [KVC, HID], F32, kind="ExternalInput")
    wuv_in = dt("wuv", [KVC, HID], F32, kind="ExternalInput")
    wqr_in = dt("wqr", [D, RD], F32, kind="ExternalInput")
    wkr_in = dt("wkr", [D, RD], F32, kind="ExternalInput")
    wo_in = dt("wo", [HID, HID], F32, kind="ExternalInput")
    b_in = {nm: dt(nm, [128, n], F32, kind="ExternalInput")
            for nm, n in [("bdkv", 4), ("bdq", 12), ("buq", 16), ("buk", 16),
                          ("bo", 16), ("buv", 16)]}

    kv_cin = dt("kv_cin", [4096, T], BF16)
    kv_cout = dt("kv_cout", [4 * 4096, T], BF16)

    ctx = ExitStack()
    En = ctx.enter_context

    x_fm = En(nc.sbuf_tensor("x_sb", [128, HC * T], F32))
    xln = En(nc.sbuf_tensor("xln", [128, HC * T], BF16))
    wa = En(nc.sbuf_tensor("warena", [128, 24576], BF16))
    sc2 = En(nc.sbuf_tensor("scratch2", [128, 7168], BF16))
    cq_sb = En(nc.sbuf_tensor("cq_sb", [128, 12 * T], BF16))
    A0 = En(nc.sbuf_tensor("A0", [128, T], F32))
    A1 = En(nc.sbuf_tensor("A1", [128, T], F32))
    B0 = A1[0:16, :]
    b0bf = En(nc.sbuf_tensor("b0bf", [16, T], BF16))
    identb = En(nc.sbuf_tensor("identb", [16, 16], BF16))
    row0 = En(nc.sbuf_tensor("row0", [1, 1536], F32))
    psacc = A0
    cosq = En(nc.sbuf_tensor("cosq_sb", [HALF, T], BF16))
    sinq = En(nc.sbuf_tensor("sinq_sb", [HALF, T], BF16))
    comb_tm = En(nc.sbuf_tensor("comb_tm", [128, 128], F32))
    small = En(nc.sbuf_tensor("small", [128, 64], F32))
    ident = En(nc.sbuf_tensor("ident_sb", [128, 128], F32))
    ones_bf = En(nc.sbuf_tensor("ones_bf", [128, 1], BF16))
    ones_f = En(nc.sbuf_tensor("ones_f", [128, 1], F32))
    onesrow = En(nc.sbuf_tensor("onesrow", [1, 128], F32))
    halfrow = En(nc.sbuf_tensor("halfrow", [1, 128], F32))
    gate_wT = En(nc.sbuf_tensor("gate_wT_sb", [128, HC * E], BF16))
    sgwT = En(nc.sbuf_tensor("sgwT_sb", [128, HC], BF16))
    wqr = En(nc.sbuf_tensor("wqr_sb", [128, RD], BF16))
    wkr = En(nc.sbuf_tensor("wkr_sb", [128, RD], BF16))
    bias = {nm: En(nc.sbuf_tensor("b_" + nm, [128, n], F32))
            for nm, n in [("bdkv", 4), ("bdq", 12), ("buq", 16), ("buk", 16),
                          ("bo", 16), ("buv", 16)]}

    HMID_O, GS_O, U1_O, XSQ_O = 0, 4096, 5120, 6144
    CKV_O, EB_O, QEFF_O, SQQ_O = 0, 0, 2048, 3072
    KEFFO_O, KHEAD_O, QHEAD_O = 4096, 5120, 6144

    RING_OFF = [0, 4096]
    W2_OFF = [8192, 16384]
    MID_O = 8192
    WDKV_O, WDQ_O, WUK_O, WUV_O = 0, 0, 0, 8192
    WUQH_O = [0, 1536]
    CTX_O = 3072
    WO_O = [19456, 21504]

    s_in = Sem(ctx, nc, "dma_in", 16)
    s_out = Sem(ctx, nc, "dma_out", 16)
    s_ring = [Sem(ctx, nc, f"ring{i}", 16) for i in range(2)]
    s_w2 = [Sem(ctx, nc, f"w2_{i}", 16) for i in range(2)]
    s_aw = [Sem(ctx, nc, f"aw{i}", 16) for i in range(4)]
    s_kb = [Sem(ctx, nc, f"kb{i}", 16) for i in range(2)]
    s_vb = [Sem(ctx, nc, f"vb{i}", 16) for i in range(2)]
    s_t = Sem(ctx, nc, "t_main", 1)
    s_v = Sem(ctx, nc, "v_main", 1)
    s_s = Sem(ctx, nc, "s_main", 1)
    s_cc = Sem(ctx, nc, "cc", 1)

    p = P(nc, ctx)
    on = p.on
    ps = p.ps

    def sync_load(dst_ap, src_ap, wait_t=None):
        val = s_in.bump()

        def f(sy, dst_ap=dst_ap, src_ap=src_ap, wait_t=wait_t):
            if wait_t is not None:
                sy.wait_ge(wait_t[0].h, wait_t[1])
            sy.dma_start(out=dst_ap, in_=src_ap).then_inc(s_in.h, 16)
        on("sync", f)
        return val

    def sync_store(dst_ap, src_ap, wait):
        val = s_out.bump()

        def f(sy, dst_ap=dst_ap, src_ap=src_ap, wait=wait):
            for sem, v in wait:
                sy.wait_ge(sem.h, v)
            sy.dma_start(out=dst_ap, in_=src_ap).then_inc(s_out.h, 16)
        on("sync", f)
        return val

    def g_load(sem, dst_ap, src_ap, wait_t=None, slow=False):
        val = sem.bump()

        def f(g, dst_ap=dst_ap, src_ap=src_ap, wait_t=wait_t, sem=sem, slow=slow):
            if wait_t is not None:
                g.wait_ge(wait_t[0].h, wait_t[1])
            g.dma_start(out=dst_ap, in_=src_ap,
                        allow_slow_non_contiguous=slow).then_inc(sem.h, 16)
        on("gpsimd", f)
        return val

    def v_run(fn, *, inc=False):
        if inc:
            val = s_v.bump()

            def f(v, fn=fn):
                fn(v)
                v.drain().then_inc(s_v.h, 1)
            on("vector", f)
            return val
        on("vector", fn)
        return None

    def s_run(fn, *, inc=False):
        if inc:
            val = s_s.bump()

            def f(sc, fn=fn):
                fn(sc)
                sc.drain().then_inc(s_s.h, 1)
            on("scalar", f)
            return val
        on("scalar", fn)
        return None

    acc_state = {"slot": 0, "guard": [None, None]}

    def emit_acc(bank, dst_lo, tv, bias_col=None):
        slot = acc_state["slot"]
        acc_state["slot"] = 1 - slot

        def fad(v, bank=bank, slot=slot, tv=tv, dst_lo=dst_lo, bias_col=bias_col):
            v.wait_ge(s_t.h, tv)
            if bias_col is None:
                v.tensor_add(psacc[:, slot * TH:(slot + 1) * TH],
                             x_fm[:, dst_lo:dst_lo + TH], ps[bank][:, :])
            else:
                v.scalar_tensor_tensor(psacc[:, slot * TH:(slot + 1) * TH],
                                       ps[bank][:, :], bias_col,
                                       x_fm[:, dst_lo:dst_lo + TH],
                                       ALU.add, ALU.add)
            v.drain()
            v.tensor_copy(x_fm[:, dst_lo:dst_lo + TH],
                          psacc[:, slot * TH:(slot + 1) * TH])
            v.drain()
        va = v_run(fad, inc=True)
        p.guard(bank, s_v, va)
        return va

    v_x = sync_load(x_fm[:, :].rearrange("p (hc t) -> p hc t", t=T),
                    x_in[:, :].rearrange("(hc p) t -> p hc t", p=128))
    sync_load(ident[:, :], ident_in[:, :])
    for nm in bias:
        sync_load(bias[nm][:, :], b_in[nm][:, :])
    v_pre_in = s_in.v

    g_load(s_aw[3], identb[:, :], ident_in[0:16, 0:16])
    g_load(s_aw[3], cosq[:, :], cosq_in[:, :])
    g_load(s_aw[3], sinq[:, :], sinq_in[:, :])
    g_load(s_aw[3], gate_wT[:, :], gate_wT_in[:, :])
    g_load(s_aw[3], sgwT[:, :], sgwT_in[:, :])
    g_load(s_aw[3], wqr[:, :], wqr_in[:, :])
    v_smallw = g_load(s_aw[3], wkr[:, :], wkr_in[:, :])

    def f_const(v):
        v.memset(ones_bf[:, :], 1.0)
        v.memset(ones_f[:, :], 1.0)
        v.memset(onesrow[:, :], 1.0)
        v.memset(halfrow[:, :], 0.5)
    v_const = v_run(f_const, inc=True)

    def emit_ln():
        x_ready = [(s_in, v_pre_in), (s_v, s_v.v), (s_v, v_const)]
        t_base = s_t.v
        for th in range(2):
            t0 = th * TH
            sq_cons = {}
            for hcb in range(HC):
                slot = hcb % 2
                slot_free = sq_cons.get(hcb - 2, t_base)

                def fs(sc, hcb=hcb, slot=slot, t0=t0, slot_free=slot_free,
                       first=(hcb == 0 and th == 0)):
                    sc.wait_ge(s_t.h, slot_free)
                    if first:
                        for sem, val in x_ready:
                            sc.wait_ge(sem.h, val)
                    ins = sc.activation(
                        sc2[:, XSQ_O + slot * TH: XSQ_O + (slot + 1) * TH],
                        x_fm[:, hcb * T + t0: hcb * T + t0 + TH], AF.Square)
                    ins.then_inc(s_s.h, 1)
                on("scalar", fs)
                sqv = s_s.bump()
                w = [(s_s, sqv)]
                if hcb == 0:
                    w = w + list(x_ready)
                    if p.bank_guard[4] is not None:
                        w.append(p.bank_guard[4]); p.bank_guard[4] = None
                    if p.bank_guard[5] is not None:
                        w.append(p.bank_guard[5]); p.bank_guard[5] = None
                p.mm(None, ps[4][0:1, :], ones_f[:, :],
                     x_fm[:, hcb * T + t0: hcb * T + t0 + TH],
                     hcb == 0, hcb == HC - 1, wait=(w if hcb == 0 else None))
                p.mm(None, ps[5][0:1, :], ones_bf[:, :],
                     sc2[:, XSQ_O + slot * TH: XSQ_O + (slot + 1) * TH],
                     hcb == 0, hcb == HC - 1,
                     inc=s_t, wait=[(s_s, sqv)])
                sq_cons[hcb] = s_t.v
            t_base = s_t.v
            tv = s_t.v

            def frow(v, tv=tv):
                v.wait_ge(s_t.h, tv)
                v.tensor_scalar_mul(row0[0:1, 0:TH], ps[4][0:1, :], 1.0 / HID)
                v.tensor_scalar_mul(row0[0:1, TH:2 * TH], ps[5][0:1, :], 1.0 / HID)
                v.drain()
                v.tensor_mul(row0[0:1, 2 * TH:3 * TH], row0[0:1, 0:TH],
                             row0[0:1, 0:TH])
                v.drain()
                v.tensor_sub(row0[0:1, TH:2 * TH], row0[0:1, TH:2 * TH],
                             row0[0:1, 2 * TH:3 * TH])
                v.drain()
                v.tensor_scalar_add(row0[0:1, TH:2 * TH], row0[0:1, TH:2 * TH],
                                    EPS)
                v.drain()
                v.reciprocal(row0[0:1, 2 * TH:3 * TH], row0[0:1, TH:2 * TH])
            v_rowa = v_run(frow, inc=True)
            p.guard(4, s_v, v_rowa)
            p.guard(5, s_v, v_rowa)

            def fsq(sc, v_rowa=v_rowa):
                sc.wait_ge(s_v.h, v_rowa)
                sc.activation(row0[0:1, TH:2 * TH], row0[0:1, 2 * TH:3 * TH],
                              AF.Sqrt)
            sv = s_run(fsq, inc=True)

            def fa(v, sv=sv):
                v.wait_ge(s_s.h, sv)
                v.tensor_mul(row0[0:1, 2 * TH:3 * TH], row0[0:1, 0:TH],
                             row0[0:1, TH:2 * TH])
            v_a = v_run(fa, inc=True)
            p.mm(6, ps[6][:, :], onesrow[0:1, :], row0[0:1, TH:2 * TH],
                 True, True, wait=[(s_v, v_a)])
            p.mm(7, ps[7][:, :], onesrow[0:1, :], row0[0:1, 2 * TH:3 * TH],
                 True, True, inc=s_t)
            tv2 = s_t.v

            def fbc(v, tv2=tv2):
                v.wait_ge(s_t.h, tv2)
                v.tensor_copy(A0[:, 0:TH], ps[6][:, :])
                v.tensor_copy(A1[:, 0:TH], ps[7][:, :])
            v_bc = v_run(fbc, inc=True)
            p.guard(6, s_v, v_bc)
            p.guard(7, s_v, v_bc)

            for hcb in range(HC):
                def fap(v, hcb=hcb, t0=t0):
                    sl = slice(hcb * T + t0, hcb * T + t0 + TH)
                    v.tensor_mul(A1[:, TH:2 * TH], x_fm[:, sl], A0[:, 0:TH])
                    v.drain()
                    v.tensor_sub(xln[:, sl], A1[:, TH:2 * TH], A1[:, 0:TH])
                    v.drain()
                on("vector", fap)

            def fd(v):
                pass
            last = v_run(lambda v: None, inc=True)
        return last

    def emit_router(v_xln):
        for th in range(2):
            bank = th
            for hcb in range(HC):
                w = None
                if hcb == 0:
                    w = [(s_v, v_xln), (s_aw[3], v_smallw)]
                p.mm(bank if hcb == 0 else None, ps[bank][0:16, :],
                     gate_wT[:, hcb * E:(hcb + 1) * E],
                     xln[:, hcb * T + th * TH: hcb * T + th * TH + TH],
                     hcb == 0, hcb == HC - 1,
                     inc=(s_t if hcb == HC - 1 else None), wait=w)
            tv = s_t.v

            def fe(v, tv=tv, th=th, bank=bank):
                v.wait_ge(s_t.h, tv)
                v.tensor_copy(B0[0:16, th * TH:(th + 1) * TH], ps[bank][0:16, :])
            ve = v_run(fe, inc=True)
            p.guard(bank, s_v, ve)
        v_lg = s_v.v
        for jb in range(8):
            p.transpose(2 if jb == 0 else None,
                        ps[2][:, jb * 16:(jb + 1) * 16],
                        B0[0:16, jb * 128:(jb + 1) * 128], ident[0:16, 0:16],
                        inc=(s_t if jb == 7 else None),
                        wait=([(s_v, v_lg)] if jb == 0 else None))
        tv_tr = s_t.v

        def fmx(v, tv_tr=tv_tr):
            v.wait_ge(s_t.h, tv_tr)
            lg3 = ps[2][:, 0:128].rearrange("p (g e) -> p g e", e=16)
            v.reduce_max(small[:, 0:8], lg3, axis=AX)
            v.drain()
            v.tensor_scalar_mul(small[:, 8:16], small[:, 0:8], -1.0)
        vmx = v_run(fmx, inc=True)
        for g8 in range(8):
            def fexp(sc, g8=g8, vmx=vmx):
                if g8 == 0:
                    sc.wait_ge(s_v.h, vmx)
                sc.activation(comb_tm[:, g8 * 16:(g8 + 1) * 16],
                              ps[2][:, g8 * 16:(g8 + 1) * 16], AF.Exp,
                              bias=small[:, 8 + g8:9 + g8], scale=1.0)
            on("scalar", fexp)
        sv = s_run(lambda sc: None, inc=True)
        p.guard(2, s_s, sv)

        def ftop(v, sv=sv):
            v.wait_ge(s_s.h, sv)
            rt3 = comb_tm[:, 0:128].rearrange("p (g e) -> p g e", e=16)
            v.reduce_max(small[:, 0:8], rt3, axis=AX)
            v.drain()
            for g8 in range(8):
                v.tensor_scalar(A0[:, g8 * 16:(g8 + 1) * 16],
                                comb_tm[:, g8 * 16:(g8 + 1) * 16],
                                small[:, g8:g8 + 1], None, ALU.is_lt)
            v.drain()
            v.tensor_mul(A0[:, 0:128], comb_tm[:, 0:128], A0[:, 0:128])
            v.drain()
            v.reduce_max(small[:, 8:16],
                         A0[:, 0:128].rearrange("p (g e) -> p g e", e=16), axis=AX)
            v.drain()
            for g8 in range(8):
                v.tensor_scalar(A0[:, g8 * 16:(g8 + 1) * 16],
                                comb_tm[:, g8 * 16:(g8 + 1) * 16],
                                small[:, 8 + g8:9 + g8], None, ALU.is_ge)
            v.drain()
            v.tensor_mul(A0[:, 0:128], comb_tm[:, 0:128], A0[:, 0:128])
            v.tensor_add(small[:, 0:8], small[:, 0:8], small[:, 8:16])
            v.drain()
            v.reciprocal(small[:, 0:8], small[:, 0:8])
            v.drain()
            for g8 in range(8):
                v.tensor_scalar_mul(comb_tm[:, g8 * 16:(g8 + 1) * 16],
                                    A0[:, g8 * 16:(g8 + 1) * 16],
                                    small[:, g8:g8 + 1])
        vtop = v_run(ftop, inc=True)
        for jb in range(8):
            bank = 3 if jb < 4 else 4
            col = (jb % 4) * 128
            p.transpose(bank if (jb % 4 == 0) else None,
                        ps[bank][0:16, col:col + 128],
                        comb_tm[:, jb * 16:(jb + 1) * 16], ident[:, :],
                        inc=(s_t if jb == 7 else None),
                        wait=([(s_v, vtop)] if jb == 0 else None))
        tv2 = s_t.v

        def fcfm(v, tv2=tv2):
            v.wait_ge(s_t.h, tv2)
            v.tensor_copy(b0bf[0:16, 0:TH], ps[3][0:16, 0:TH])
            v.tensor_copy(b0bf[0:16, TH:T], ps[4][0:16, 0:TH])
        vfm = v_run(fcfm, inc=True)
        p.guard(3, s_v, vfm)
        p.guard(4, s_v, vfm)
        return vfm

    def emit_ffn():
        v_xln = emit_ln()
        if _FFN_CUT == "ln":
            return s_v.v
        v_comb = emit_router(v_xln)
        if _FFN_CUT == "router":
            return s_v.v

        ring_cons = [None, None]
        ring_fill = [0, 0]

        def ring_load(slot, src_ap):
            wt = None
            if ring_cons[slot] is not None:
                wt = (s_t, ring_cons[slot])
            val = g_load(s_ring[slot],
                         wa[:, RING_OFF[slot]:RING_OFF[slot] + 4096]
                         .rearrange("p (hc i) -> p hc i", i=256),
                         src_ap, wait_t=wt)
            ring_fill[slot] += 1
            return val

        ring_srcs = []
        for e in range(E):
            for pc in range(4):
                ring_srcs.append(w13_in[e, pc].rearrange("(hc p) i -> p hc i", p=128))
        for half in range(2):
            for pc in range(8):
                ring_srcs.append(wsgu_in[half * 8 + pc]
                                 .rearrange("(hc p) i -> p hc i", p=128))
            for dp in range(8):
                ring_srcs.append(wsd_in[dp].rearrange("(ic p) h -> p ic h", p=128))
        ring_vals = {}
        ridx = 0

        def ring_issue():
            nonlocal ridx
            slot = ridx % 2
            ring_vals[ridx] = (slot, ring_load(slot, ring_srcs[ridx]))
            ridx += 1

        if _FFN_CUT != "e1nodma":
            ring_issue()
            ring_issue()

        w2_cons = [None, None]

        def w2_load(bufi, e):
            wt = (s_t, w2_cons[bufi]) if w2_cons[bufi] is not None else None
            return g_load(s_w2[bufi],
                          wa[:, W2_OFF[bufi]:W2_OFF[bufi] + 4 * HID]
                          .rearrange("p (ic h) -> p ic h", h=HID),
                          w2T_in[e].rearrange("(ic p) h -> p ic h", p=128),
                          wait_t=wt)

        w2_vals = [w2_load(0, 0) if _FFN_CUT != "e1nodma" else None, None]
        gs_guard = [None, None]

        if _FFN_CUT == "e1nocb":
            return s_v.v
        n_exp = 1 if _FFN_CUT.startswith("e1") else E
        if _FFN_CUT == "e1x":
            n_exp = 1
        for e in range(n_exp):
            p.mm(5, ps[5][0:1, :], identb[0:16, e:e + 1], b0bf[0:16, 0:TH],
                 True, True,
                 wait=([(s_v, v_comb), (s_aw[3], v_smallw)] if e == 0 else None))
            p.mm(6, ps[6][0:1, :], identb[0:16, e:e + 1], b0bf[0:16, TH:T],
                 True, True, inc=s_t)
            tsel = s_t.v

            def fsel(sc, tsel=tsel):
                sc.wait_ge(s_t.h, tsel)
                sc.activation(row0[0:1, 0:TH], ps[5][0:1, :], AF.Copy)
                sc.activation(row0[0:1, TH:2 * TH], ps[6][0:1, :], AF.Copy)
            ssel = s_run(fsel, inc=True)
            p.guard(5, s_s, ssel)
            p.guard(6, s_s, ssel)
            p.mm(5, ps[5][:, :], halfrow[0:1, :], row0[0:1, 0:TH], True, True,
                 wait=[(s_s, ssel)])
            p.mm(6, ps[6][:, :], halfrow[0:1, :], row0[0:1, TH:2 * TH],
                 True, True, inc=s_t)
            tcb = s_t.v

            def fcb(v, tcb=tcb):
                v.wait_ge(s_t.h, tcb)
                v.tensor_copy(A0[:, 0:TH], ps[5][:, :])
                v.tensor_copy(A0[:, TH:T], ps[6][:, :])
            v_cb = v_run(fcb, inc=True)
            p.guard(5, s_v, v_cb)
            p.guard(6, s_v, v_cb)
            if _FFN_CUT in ("e1a", "e1nodma"):
                return s_v.v

            for ic in range(4):
                pci = (0 if ic < 2 else 2)
                coloff = (ic % 2) * 128
                r1 = ring_vals[e * 4 + pci]
                r3 = ring_vals[e * 4 + pci + 1]
                for th in range(2):
                    bg = ic % 2
                    bu = 2 + ic % 2
                    for hcb in range(HC):
                        w = None
                        if hcb == 0:
                            w = [(s_ring[r1[0]], r1[1])]
                            if ic == 0 and th == 0:
                                w.append((s_v, v_xln))
                        p.mm(bg if hcb == 0 else None, ps[bg][:, :],
                             wa[:, RING_OFF[r1[0]] + hcb * 256 + coloff:
                                RING_OFF[r1[0]] + hcb * 256 + coloff + 128],
                             xln[:, hcb * T + th * TH: hcb * T + th * TH + TH],
                             hcb == 0, hcb == HC - 1,
                             inc=(s_t if hcb == HC - 1 else None), wait=w)
                    tg = s_t.v

                    def fsilu(sc, tg=tg, bg=bg, th=th, gfree=gs_guard[th]):
                        sc.wait_ge(s_t.h, tg)
                        if gfree is not None:
                            sc.wait_ge(s_v.h, gfree)
                        sc.activation(sc2[:, GS_O + th * TH: GS_O + (th + 1) * TH],
                                      ps[bg][:, :], AF.Silu)
                    sgv = s_run(fsilu, inc=True)
                    p.guard(bg, s_s, sgv)
                    if _FFN_CUT == "e1c":
                        gs_guard[th] = None
                        continue
                    for hcb in range(HC):
                        w = [(s_ring[r3[0]], r3[1])] if hcb == 0 else None
                        p.mm(bu if hcb == 0 else None, ps[bu][:, :],
                             wa[:, RING_OFF[r3[0]] + hcb * 256 + coloff:
                                RING_OFF[r3[0]] + hcb * 256 + coloff + 128],
                             xln[:, hcb * T + th * TH: hcb * T + th * TH + TH],
                             hcb == 0, hcb == HC - 1,
                             inc=(s_t if hcb == HC - 1 else None), wait=w)
                    tu = s_t.v

                    def fmid(v, tu=tu, sgv=sgv, bu=bu, ic=ic, th=th, v_cb=v_cb):
                        v.wait_ge(s_t.h, tu)
                        v.wait_ge(s_s.h, sgv)
                        v.wait_ge(s_v.h, v_cb)
                        v.tensor_mul(sc2[:, U1_O: U1_O + TH],
                                     ps[bu][:, :], A0[:, th * TH:(th + 1) * TH])
                        v.drain()
                        v.tensor_mul(
                            sc2[:, HMID_O + ic * T + th * TH:
                                HMID_O + ic * T + th * TH + TH],
                            sc2[:, U1_O: U1_O + TH],
                            sc2[:, GS_O + th * TH: GS_O + (th + 1) * TH])
                        v.drain()
                    vmid = v_run(fmid, inc=True)
                    p.guard(bu, s_v, vmid)
                    gs_guard[th] = vmid
                if ic in (1, 3):
                    ring_cons[r1[0]] = s_t.v
                    ring_cons[r3[0]] = s_t.v
                    if ridx < len(ring_srcs):
                        ring_issue()
                    if ridx < len(ring_srcs):
                        ring_issue()
            v_hmid = s_v.v
            if _FFN_CUT in ("e1b", "e1c"):
                return s_v.v

            bufi = e % 2
            w2v = w2_vals[bufi]
            if e + 1 < n_exp:
                w2_vals[1 - bufi] = w2_load(1 - bufi, e + 1)
            last_w2_mm = None
            for hcb in range(HC):
                for th in range(2):
                    bm = 4 + (hcb * 2 + th) % 2
                    for ic in range(4):
                        w = None
                        if hcb == 0 and th == 0 and ic == 0:
                            w = [(s_w2[bufi], w2v), (s_v, v_hmid)]
                        p.mm(bm if ic == 0 else None, ps[bm][:, :],
                             wa[:, W2_OFF[bufi] + ic * HID + hcb * 128:
                                W2_OFF[bufi] + ic * HID + hcb * 128 + 128],
                             sc2[:, HMID_O + ic * T + th * TH:
                                 HMID_O + ic * T + th * TH + TH],
                             ic == 0, ic == 3,
                             inc=(s_t if ic == 3 else None), wait=w)
                    tv = s_t.v
                    if _FFN_CUT == "e1k1":
                        if hcb == 0 and th == 0:
                            emit_acc(bm, hcb * T + th * TH, tv)
                        else:
                            def fdisc(v, tv=tv, bm=bm):
                                v.wait_ge(s_t.h, tv)
                                v.tensor_copy(psacc[:, 0:TH], ps[bm][:, :])
                                v.drain()
                            vd = v_run(fdisc, inc=True)
                            p.guard(bm, s_v, vd)
                    elif _FFN_CUT != "e1m":
                        emit_acc(bm, hcb * T + th * TH, tv)
                    else:
                        def fdisc(v, tv=tv, bm=bm):
                            v.wait_ge(s_t.h, tv)
                            v.tensor_copy(psacc[:, 0:TH], ps[bm][:, :])
                            v.drain()
                        vd = v_run(fdisc, inc=True)
                        p.guard(bm, s_v, vd)
            w2_cons[bufi] = s_t.v
        if _FFN_CUT.startswith("e1") or _FFN_CUT == "e16":
            return s_v.v

        for th in range(2):
            bank = th
            for hcb in range(HC):
                p.mm(bank if hcb == 0 else None, ps[bank][0:1, :],
                     sgwT[:, hcb:hcb + 1],
                     xln[:, hcb * T + th * TH: hcb * T + th * TH + TH],
                     hcb == 0, hcb == HC - 1,
                     inc=(s_t if hcb == HC - 1 else None))
            tv = s_t.v

            def fsig(sc, tv=tv, th=th, bank=bank):
                sc.wait_ge(s_t.h, tv)
                sc.activation(row0[0:1, th * TH:(th + 1) * TH],
                              ps[bank][0:1, :], AF.Sigmoid)
            ssg = s_run(fsig, inc=True)
            p.guard(bank, s_s, ssg)
        sv_gate = s_s.v
        p.mm(5, ps[5][:, :], halfrow[0:1, :], row0[0:1, 0:TH], True, True,
             wait=[(s_s, sv_gate)])
        p.mm(6, ps[6][:, :], halfrow[0:1, :], row0[0:1, TH:T], True, True,
             inc=s_t)
        tgb = s_t.v

        def fgbc(v, tgb=tgb):
            v.wait_ge(s_t.h, tgb)
            v.tensor_copy(A1[:, 0:TH], ps[5][:, :])
            v.tensor_copy(A1[:, TH:T], ps[6][:, :])
        v_gbc = v_run(fgbc, inc=True)
        p.guard(5, s_v, v_gbc)
        p.guard(6, s_v, v_gbc)

        for half in range(2):
            for isc8 in range(8):
                isc = half * 8 + isc8
                gidx = 64 + half * 16 + 2 * (isc8 // 2)
                uidx = gidx + 1
                coloff = (isc % 2) * 128
                if isc % 2 == 0:
                    while ridx <= uidx:
                        ring_issue()
                rg = ring_vals[gidx]
                ru = ring_vals[uidx]
                for th in range(2):
                    bg = isc8 % 2
                    bu = 2 + isc8 % 2
                    for hcb in range(HC):
                        w = None
                        if hcb == 0:
                            w = [(s_ring[rg[0]], rg[1])]
                            if half == 0 and isc8 == 0 and th == 0:
                                if w2_cons[0] is not None:
                                    w.append((s_t, w2_cons[0]))
                        p.mm(bg if hcb == 0 else None, ps[bg][:, :],
                             wa[:, RING_OFF[rg[0]] + hcb * 256 + coloff:
                                RING_OFF[rg[0]] + hcb * 256 + coloff + 128],
                             xln[:, hcb * T + th * TH: hcb * T + th * TH + TH],
                             hcb == 0, hcb == HC - 1,
                             inc=(s_t if hcb == HC - 1 else None), wait=w)
                    tg = s_t.v

                    def fsilu(sc, tg=tg, bg=bg, th=th, gfree=gs_guard[th]):
                        sc.wait_ge(s_t.h, tg)
                        if gfree is not None:
                            sc.wait_ge(s_v.h, gfree)
                        sc.activation(sc2[:, GS_O + th * TH: GS_O + (th + 1) * TH],
                                      ps[bg][:, :], AF.Silu)
                    sgv = s_run(fsilu, inc=True)
                    p.guard(bg, s_s, sgv)
                    for hcb in range(HC):
                        w = [(s_ring[ru[0]], ru[1])] if hcb == 0 else None
                        p.mm(bu if hcb == 0 else None, ps[bu][:, :],
                             wa[:, RING_OFF[ru[0]] + hcb * 256 + coloff:
                                RING_OFF[ru[0]] + hcb * 256 + coloff + 128],
                             xln[:, hcb * T + th * TH: hcb * T + th * TH + TH],
                             hcb == 0, hcb == HC - 1,
                             inc=(s_t if hcb == HC - 1 else None), wait=w)
                    tu = s_t.v

                    def fmid(v, tu=tu, sgv=sgv, bu=bu, isc8=isc8, th=th,
                             v_gbc=v_gbc):
                        v.wait_ge(s_t.h, tu)
                        v.wait_ge(s_s.h, sgv)
                        v.wait_ge(s_v.h, v_gbc)
                        v.tensor_mul(sc2[:, U1_O: U1_O + TH], ps[bu][:, :],
                                     A1[:, th * TH:(th + 1) * TH])
                        v.drain()
                        v.tensor_mul(
                            wa[:, MID_O + isc8 * T + th * TH:
                               MID_O + isc8 * T + th * TH + TH],
                            sc2[:, U1_O: U1_O + TH],
                            sc2[:, GS_O + th * TH: GS_O + (th + 1) * TH])
                        v.drain()
                    vmid = v_run(fmid, inc=True)
                    p.guard(bu, s_v, vmid)
                    gs_guard[th] = vmid
                if isc % 2 == 1:
                    ring_cons[rg[0]] = s_t.v
                    ring_cons[ru[0]] = s_t.v
            v_mid = s_v.v
            for dp in range(8):
                dn_idx = 64 + half * 16 + 8 + dp
                while ridx <= dn_idx:
                    ring_issue()
                rd_ = ring_vals[dn_idx]
                for hh in range(2):
                    hcb = dp * 2 + hh
                    for th in range(2):
                        bm = 4 + (hcb * 2 + th) % 2
                        for i8 in range(8):
                            isc = half * 8 + i8
                            w = None
                            if hh == 0 and th == 0 and i8 == 0:
                                w = [(s_ring[rd_[0]], rd_[1]), (s_v, v_mid)]
                            p.mm(bm if i8 == 0 else None, ps[bm][:, :],
                                 wa[:, RING_OFF[rd_[0]] + isc * 256 + hh * 128:
                                    RING_OFF[rd_[0]] + isc * 256 + hh * 128 + 128],
                                 wa[:, MID_O + i8 * T + th * TH:
                                    MID_O + i8 * T + th * TH + TH],
                                 i8 == 0, i8 == 7,
                                 inc=(s_t if i8 == 7 else None), wait=w)
                        tv = s_t.v
                        emit_acc(bm, hcb * T + th * TH, tv)
                ring_cons[rd_[0]] = s_t.v
            if half == 0:
                pass
        return s_v.v

    def emit_attention():
        v_xln = emit_ln()

        v_wdkv = g_load(s_aw[0], wa[:, WDKV_O:WDKV_O + 8192]
                        .rearrange("p (hc k) -> p hc k", k=KVC),
                        wdkv_in[:, :].rearrange("(hc p) k -> p hc k", p=128))
        for kvcc in range(4):
            for th in range(2):
                bank = (kvcc * 2 + th) % 4
                for hcb in range(HC):
                    w = None
                    if hcb == 0:
                        w = [(s_aw[0], v_wdkv)]
                        if kvcc == 0 and th == 0:
                            w.append((s_v, v_xln))
                    p.mm(bank if hcb == 0 else None, ps[bank][:, :],
                         wa[:, WDKV_O + hcb * KVC + kvcc * 128:
                            WDKV_O + hcb * KVC + kvcc * 128 + 128],
                         xln[:, hcb * T + th * TH: hcb * T + th * TH + TH],
                         hcb == 0, hcb == HC - 1,
                         inc=(s_t if hcb == HC - 1 else None), wait=w)
                tv = s_t.v

                def fevac(sc, tv=tv, kvcc=kvcc, th=th, bank=bank):
                    sc.wait_ge(s_t.h, tv)
                    sc.activation(sc2[:, CKV_O + kvcc * T + th * TH:
                                      CKV_O + kvcc * T + th * TH + TH],
                                  ps[bank][:, :], AF.Identity,
                                  bias=bias["bdkv"][:, kvcc:kvcc + 1], scale=1.0)
                sev = s_run(fevac, inc=True)
                p.guard(bank, s_s, sev)
        v_ckv_t = s_t.v
        v_ckv = s_s.v

        v_wdq = g_load(s_aw[1], wa[:, 0:24576].rearrange("p (hc q) -> p hc q", q=QC),
                       wdq_in[:, :].rearrange("(hc p) q -> p hc q", p=128),
                       wait_t=(s_t, v_ckv_t))
        for qcc in range(12):
            for th in range(2):
                bank = (qcc * 2 + th) % 4
                for hcb in range(HC):
                    w = [(s_aw[1], v_wdq)] if hcb == 0 else None
                    p.mm(bank if hcb == 0 else None, ps[bank][:, :],
                         wa[:, hcb * QC + qcc * 128: hcb * QC + qcc * 128 + 128],
                         xln[:, hcb * T + th * TH: hcb * T + th * TH + TH],
                         hcb == 0, hcb == HC - 1,
                         inc=(s_t if hcb == HC - 1 else None), wait=w)
                tv = s_t.v

                def fevac(sc, tv=tv, qcc=qcc, th=th, bank=bank):
                    sc.wait_ge(s_t.h, tv)
                    sc.activation(cq_sb[:, qcc * T + th * TH:
                                        qcc * T + th * TH + TH],
                                  ps[bank][:, :], AF.Identity,
                                  bias=bias["bdq"][:, qcc:qcc + 1], scale=1.0)
                sev = s_run(fevac, inc=True)
                p.guard(bank, s_s, sev)
        v_cq_t = s_t.v

        v_wuk = g_load(s_aw[2], wa[:, WUK_O:WUK_O + 8192]
                       .rearrange("p (kc h) -> p kc h", h=HID),
                       wuk_in[:, :].rearrange("(kc p) h -> p kc h", p=128),
                       wait_t=(s_t, v_cq_t))
        v_wuv = g_load(s_aw[0], wa[:, WUV_O:WUV_O + 8192]
                       .rearrange("p (kc h) -> p kc h", h=HID),
                       wuv_in[:, :].rearrange("(kc p) h -> p kc h", p=128))

        v_store_vals = []
        for tc in range(8):
            for dts in range(4):
                bank = (tc * 4 + dts) % 4
                for kvcc in range(4):
                    w = None
                    if tc == 0 and dts == 0 and kvcc == 0:
                        w = [(s_aw[0], v_wuv), (s_s, v_ckv), (s_t, v_cq_t)]
                    elif kvcc == 0:
                        w = []
                    p.mm(bank if kvcc == 0 else None, ps[bank][:, :],
                         sc2[:, CKV_O + kvcc * T + tc * 128:
                             CKV_O + kvcc * T + tc * 128 + 128],
                         wa[:, WUV_O + kvcc * HID + dts * TH:
                            WUV_O + kvcc * HID + dts * TH + TH],
                         kvcc == 0, kvcc == 3,
                         inc=(s_t if kvcc == 3 else None), wait=w)
                tv = s_t.v

                def fvadd(sc, tv=tv, tc=tc, dts=dts, bank=bank):
                    sc.wait_ge(s_t.h, tv)
                    sc.activation(xln[:, tc * HID + dts * TH:
                                      tc * HID + dts * TH + TH],
                                  ps[bank][:, :], AF.Copy)
                vv = s_run(fvadd, inc=True)
                p.guard(bank, s_s, vv)
            v_store_vals.append(sync_store(
                kv_cin[2048 + tc * 256: 2048 + (tc + 1) * 256, :]
                .rearrange("(p dh) c -> p dh c", p=128),
                xln[:, tc * HID:(tc + 1) * HID]
                .rearrange("p (dh c) -> p dh c", c=T),
                wait=[(s_s, s_s.v)]))

        def fkz(v):
            v.memset(sc2[ROW_B:64, KEFFO_O:KEFFO_O + T], 0.0)
            v.drain()
            v.memset(sc2[ROW_B:ROW_B + 1, KEFFO_O:KEFFO_O + T], -1.0)
            v.drain()
        v_kz = v_run(fkz, inc=True)

        prev_keff_store = None
        for h in range(NH):
            for th in range(2):
                bank = th
                for kvcc in range(4):
                    w = None
                    if kvcc == 0:
                        w = [(s_aw[2], v_wuk), (s_s, v_ckv)]
                    p.mm(bank if kvcc == 0 else None, ps[bank][:, :],
                         wa[:, WUK_O + kvcc * HID + h * 128:
                            WUK_O + kvcc * HID + h * 128 + 128],
                         sc2[:, CKV_O + kvcc * T + th * TH:
                             CKV_O + kvcc * T + th * TH + TH],
                         kvcc == 0, kvcc == 3,
                         inc=(s_t if kvcc == 3 else None), wait=w)
                tv = s_t.v

                def fke(sc, tv=tv, th=th, bank=bank, h=h, pks=prev_keff_store):
                    sc.wait_ge(s_t.h, tv)
                    if th == 0 and pks is not None:
                        sc.wait_ge(s_out.h, pks)
                    sc.activation(sc2[:, KHEAD_O + th * TH: KHEAD_O + th * TH + TH],
                                  ps[bank][:, :], AF.Identity,
                                  bias=bias["buk"][:, h:h + 1], scale=1.0)
                    sc.activation(sc2[64:128, KEFFO_O + th * TH:
                                      KEFFO_O + th * TH + TH],
                                  ps[bank][64:128, :], AF.Identity,
                                  bias=bias["buk"][64:128, h:h + 1], scale=1.0)
                skv = s_run(fke, inc=True)
                p.guard(bank, s_s, skv)
            v_khead = s_s.v
            for th in range(2):
                p.mm(2, ps[2][0:HALF, :], wkr[:, 0:HALF],
                     sc2[:, KHEAD_O + th * TH: KHEAD_O + th * TH + TH],
                     True, True, wait=[(s_s, v_khead), (s_aw[3], v_smallw)])
                p.mm(3, ps[3][0:HALF, :], wkr[:, HALF:RD],
                     sc2[:, KHEAD_O + th * TH: KHEAD_O + th * TH + TH],
                     True, True, inc=s_t)
                tv = s_t.v

                def frope(v, tv=tv, th=th, h=h, v_kz=v_kz):
                    v.wait_ge(s_t.h, tv)
                    if h == 0 and th == 0:
                        v.wait_ge(s_v.h, v_kz)
                    v.tensor_mul(A0[0:HALF, 0:TH], ps[2][0:HALF, :],
                                 cosq[:, th * TH:(th + 1) * TH])
                    v.tensor_mul(A1[0:HALF, 0:TH], ps[3][0:HALF, :],
                                 sinq[:, th * TH:(th + 1) * TH])
                    v.drain()
                    v.tensor_sub(sc2[0:HALF, KEFFO_O + th * TH:
                                     KEFFO_O + th * TH + TH],
                                 A0[0:HALF, 0:TH], A1[0:HALF, 0:TH])
                    v.drain()
                vr = v_run(frope, inc=True)
                p.guard(2, s_v, vr)
                p.guard(3, s_v, vr)
            prev_keff_store = sync_store(kv_cin[h * 128:(h + 1) * 128, :],
                                         sc2[:, KEFFO_O:KEFFO_O + T],
                                         wait=[(s_v, s_v.v)])
        v_all_stores = s_out.v

        def fcc(g):
            g.wait_ge(s_out.h, v_all_stores)
            g.collective_compute(
                "AllGather", ALU.bypass,
                replica_groups=[[0, 1, 2, 3], [4, 5, 6, 7]],
                ins=[kv_cin.ap().opt()],
                outs=[kv_cout.ap().opt()],
            ).then_inc(s_cc.h, 1)
        on("gpsimd", fcc)
        v_cc = s_cc.bump()

        KB_O = [0, 4096]
        VB_O = [8192, 12288]
        kb_cons = [None, None]
        vb_cons = [None, None]
        wuqh_cons = [None, None]
        wuqh_vals = [None, None]

        def load_kb(h):
            slot = h % 2
            first_w = [(s_cc, v_cc)]
            if kb_cons[slot] is not None:
                first_w.append((s_t, kb_cons[slot]))
            vals = []
            for r in range(4):
                val = s_kb[slot].bump()

                def f(sy, r=r, h=h, slot=slot, w=(first_w if r == 0 else [])):
                    for sem, vv in w:
                        sy.wait_ge(sem.h, vv)
                    sy.dma_start(
                        out=xln[:, KB_O[slot] + r * T: KB_O[slot] + (r + 1) * T],
                        in_=kv_cout[r * 4096 + h * 128: r * 4096 + (h + 1) * 128, :])\
                        .then_inc(s_kb[slot].h, 16)
                on("sync", f)
                vals.append(val)
            return vals[-1]

        def load_vb(h):
            slot = h % 2
            first_w = [(s_cc, v_cc)]
            if vb_cons[slot] is not None:
                first_w.append((s_t, vb_cons[slot]))
            vals = []
            dh = h // 8
            hc8 = (h % 8) * 128
            for r in range(4):
                val = s_vb[slot].bump()

                def f(sy, r=r, slot=slot, dh=dh, hc8=hc8,
                      w=(first_w if r == 0 else [])):
                    for sem, vv in w:
                        sy.wait_ge(sem.h, vv)
                    src = kv_cout[r * 4096 + 2048: r * 4096 + 4096, :]\
                        .rearrange("(tc p dh) c -> p tc dh c", p=128, dh=2)
                    sy.dma_start(
                        out=xln[:, VB_O[slot] + r * T: VB_O[slot] + (r + 1) * T]
                        .rearrange("p (tc o d) -> p tc o d", o=1, d=128),
                        in_=src[:, :, dh:dh + 1, hc8:hc8 + 128])\
                        .then_inc(s_vb[slot].h, 16)
                on("sync", f)
                vals.append(val)
            return vals[-1]

        def load_wuqh(h):
            slot = h % 2
            wt = (s_t, wuqh_cons[slot]) if wuqh_cons[slot] is not None else None
            return g_load(s_aw[slot],
                          wa[:, WUQH_O[slot]:WUQH_O[slot] + 1536]
                          .rearrange("p (qc d) -> p qc d", d=128),
                          wuq_in[:, h * 128:(h + 1) * 128]
                          .rearrange("(qc p) d -> p qc d", p=128), wait_t=wt)

        kb_vals = [load_kb(0), None]
        vb_vals = [load_vb(0), None]
        wuqh_vals = [load_wuqh(0), None]
        kb_vals[1] = load_kb(1)
        vb_vals[1] = load_vb(1)
        wuqh_vals[1] = load_wuqh(1)

        wo_cons = [None, None]
        wo_vals = {}

        def load_wo(piece):
            slot = piece % 2
            wt = (s_t, wo_cons[slot]) if wo_cons[slot] is not None else None
            wo_vals[piece] = g_load(
                s_w2[slot],
                wa[:, WO_O[slot]:WO_O[slot] + 2048]
                .rearrange("p (hc c) -> p hc c", c=128),
                wo_in[:, piece * 128:(piece + 1) * 128]
                .rearrange("(hc p) c -> p hc c", p=128), wait_t=wt)

        for h in range(NH):
            slot = h % 2
            kbv, vbv, wqv = kb_vals[slot], vb_vals[slot], wuqh_vals[slot]
            kb = xln[:, KB_O[slot]:KB_O[slot] + 4096]
            vbuf = xln[:, VB_O[slot]:VB_O[slot] + 4096]
            for th in range(2):
                bank = th
                for qcc in range(12):
                    w = None
                    if qcc == 0:
                        w = [(s_aw[slot], wqv)]
                    p.mm(bank if qcc == 0 else None, ps[bank][:, :],
                         wa[:, WUQH_O[slot] + qcc * 128:
                            WUQH_O[slot] + qcc * 128 + 128],
                         cq_sb[:, qcc * T + th * TH: qcc * T + th * TH + TH],
                         qcc == 0, qcc == 11,
                         inc=(s_t if qcc == 11 else None), wait=w)
                tv = s_t.v

                def fqe(sc, tv=tv, th=th, bank=bank, h=h):
                    sc.wait_ge(s_t.h, tv)
                    sc.activation(sc2[:, QHEAD_O + th * TH: QHEAD_O + th * TH + TH],
                                  ps[bank][:, :], AF.Identity,
                                  bias=bias["buq"][:, h:h + 1], scale=1.0)
                    sc.activation(sc2[64:128, QEFF_O + th * TH:
                                      QEFF_O + th * TH + TH],
                                  ps[bank][64:128, :], AF.Identity,
                                  bias=bias["buq"][64:128, h:h + 1], scale=1.0)
                sqv = s_run(fqe, inc=True)
                p.guard(bank, s_s, sqv)
            wuqh_cons[slot] = s_t.v
            v_qhead = s_s.v
            if h + 2 < NH:
                wuqh_vals[slot] = load_wuqh(h + 2)
            def fqz(v, h=h):
                v.memset(sc2[ROW_B:64, QEFF_O:QEFF_O + T], 0.0)
                v.drain()
            v_qz = v_run(fqz, inc=True)
            for th in range(2):
                p.mm(2, ps[2][0:HALF, :], wqr[:, 0:HALF],
                     sc2[:, QHEAD_O + th * TH: QHEAD_O + th * TH + TH],
                     True, True, wait=[(s_s, v_qhead)])
                p.mm(3, ps[3][0:HALF, :], wqr[:, HALF:RD],
                     sc2[:, QHEAD_O + th * TH: QHEAD_O + th * TH + TH],
                     True, True, inc=s_t)
                tv = s_t.v

                def frope(v, tv=tv, th=th):
                    v.wait_ge(s_t.h, tv)
                    v.tensor_mul(A0[0:HALF, 0:TH], ps[2][0:HALF, :],
                                 cosq[:, th * TH:(th + 1) * TH])
                    v.tensor_mul(A1[0:HALF, 0:TH], ps[3][0:HALF, :],
                                 sinq[:, th * TH:(th + 1) * TH])
                    v.drain()
                    v.tensor_sub(sc2[0:HALF, QEFF_O + th * TH:
                                     QEFF_O + th * TH + TH],
                                 A0[0:HALF, 0:TH], A1[0:HALF, 0:TH])
                    v.drain()
                vr = v_run(frope, inc=True)
                p.guard(2, s_v, vr)
                p.guard(3, s_v, vr)
            v_qrope = s_v.v

            def fqsq(sc, v_qrope=v_qrope):
                sc.wait_ge(s_v.h, v_qrope)
                sc.activation(sc2[:, SQQ_O:SQQ_O + T], sc2[:, QEFF_O:QEFF_O + T],
                              AF.Square)
            sqsq = s_run(fqsq, inc=True)
            for th in range(2):
                p.mm(4 + th, ps[4 + th][0:1, :], ones_bf[:, :],
                     sc2[:, SQQ_O + th * TH: SQQ_O + th * TH + TH], True, True,
                     inc=(s_t if th == 1 else None),
                     wait=([(s_s, sqsq)] if th == 0 else None))
            tqn = s_t.v

            def fqn(sc, tqn=tqn):
                sc.wait_ge(s_t.h, tqn)
                sc.activation(row0[0:1, 0:TH], ps[4][0:1, :], AF.Sqrt)
                sc.activation(row0[0:1, TH:T], ps[5][0:1, :], AF.Sqrt)
            sqn = s_run(fqn, inc=True)
            p.guard(4, s_s, sqn)
            p.guard(5, s_s, sqn)
            kops = []
            for kc8 in range(8):
                if kc8 < 2:
                    slot_free = ("t", tqn)
                else:
                    slot_free = ("v", kops[kc8 - 2])

                def fks(sc, kc8=kc8, kb=kb, kbv=kbv, slot=slot,
                        slot_free=slot_free):
                    if kc8 == 0:
                        sc.wait_ge(s_kb[slot].h, kbv)
                    if slot_free[0] == "t":
                        sc.wait_ge(s_t.h, slot_free[1])
                    else:
                        sc.wait_ge(s_v.h, slot_free[1])
                    sc.activation(sc2[:, SQQ_O + (kc8 % 2) * TH:
                                      SQQ_O + (kc8 % 2) * TH + TH],
                                  kb[:, kc8 * TH:(kc8 + 1) * TH], AF.Square)
                sks = s_run(fks, inc=True)

                def fkr(v, sks=sks, kc8=kc8):
                    v.wait_ge(s_s.h, sks)
                    v.reduce_max(small[:, 1:2],
                                 sc2[:, SQQ_O + (kc8 % 2) * TH:
                                     SQQ_O + (kc8 % 2) * TH + TH], axis=AX)
                    v.drain()
                    if kc8 == 0:
                        v.tensor_copy(small[:, 0:1], small[:, 1:2])
                    else:
                        v.tensor_max(small[:, 0:1], small[:, 0:1], small[:, 1:2])
                    v.drain()
                kops.append(v_run(fkr, inc=True))
            v_kn = kops[-1]
            def fknf(v, v_kn=v_kn):
                v.tensor_copy(small[:, 2:3], small[:, 0:1])
                v.drain()
            vknf = v_run(fknf, inc=True)
            p.mm(6, ps[6][0:1, 0:1], ones_f[:, :], small[:, 2:3], True, True,
                 inc=s_t, wait=[(s_v, vknf)])
            tkm = s_t.v

            def fkm(sc, tkm=tkm):
                sc.wait_ge(s_t.h, tkm)
                sc.activation(small[0:1, 3:4], ps[6][0:1, 0:1], AF.Sqrt)
            skm = s_run(fkm, inc=True)
            p.guard(6, s_s, skm)
            p.mm(7, ps[7][ROW_B:ROW_B + 1, :], small[0:1, 3:4],
                 row0[0:1, 0:TH], True, True, inc=s_t,
                 wait=[(s_s, skm), (s_v, v_qz)])
            tbd = s_t.v

            def fbd(sc, tbd=tbd):
                sc.wait_ge(s_t.h, tbd)
                sc.activation(sc2[ROW_B:ROW_B + 1, QEFF_O:QEFF_O + TH],
                              ps[7][ROW_B:ROW_B + 1, :], AF.Copy)
            sbd1 = s_run(fbd, inc=True)
            p.guard(7, s_s, sbd1)
            p.mm(7, ps[7][ROW_B:ROW_B + 1, :], small[0:1, 3:4],
                 row0[0:1, TH:T], True, True, inc=s_t)
            tbd2 = s_t.v

            def fbd2(sc, tbd2=tbd2):
                sc.wait_ge(s_t.h, tbd2)
                sc.activation(sc2[ROW_B:ROW_B + 1, QEFF_O + TH:QEFF_O + T],
                              ps[7][ROW_B:ROW_B + 1, :], AF.Copy)
            sbd2 = s_run(fbd2, inc=True)
            p.guard(7, s_s, sbd2)
            v_qeff_ready = s_s.v

            for th in range(2):
                ctx_bank = 3
                dn_bank = 6
                exp_cons = {}
                for kc in range(32):
                    sb = kc % 3
                    w = []
                    if kc == 0:
                        w = [(s_s, v_qeff_ready), (s_kb[slot], kbv)]
                    p.mm(sb, ps[sb][:, :], kb[:, kc * 128:(kc + 1) * 128],
                         sc2[:, QEFF_O + th * TH: QEFF_O + th * TH + TH],
                         True, True, inc=s_t, wait=w)
                    t_sc = s_t.v

                    def fexp(sc, t_sc=t_sc, kc=kc, sb=sb, exw=exp_cons.get(kc - 4)):
                        sc.wait_ge(s_t.h, t_sc)
                        if exw is not None:
                            sc.wait_ge(s_t.h, exw)
                        sc.activation(sc2[:, EB_O + (kc % 4) * TH:
                                          EB_O + (kc % 4) * TH + TH],
                                      ps[sb][:, :], AF.Exp, scale=SCALE)
                    sev = s_run(fexp, inc=True)
                    p.guard(sb, s_s, sev)
                    w2_ = [(s_s, sev)]
                    if kc == 0:
                        w2_.append((s_vb[slot], vbv))
                        if p.bank_guard[ctx_bank] is not None:
                            w2_.append(p.bank_guard[ctx_bank])
                            p.bank_guard[ctx_bank] = None
                        if p.bank_guard[dn_bank] is not None:
                            w2_.append(p.bank_guard[dn_bank])
                            p.bank_guard[dn_bank] = None
                    p.mm(None, ps[ctx_bank][:, :],
                         vbuf[:, kc * 128:(kc + 1) * 128],
                         sc2[:, EB_O + (kc % 4) * TH: EB_O + (kc % 4) * TH + TH],
                         kc == 0, kc == 31, wait=w2_)
                    p.mm(None, ps[dn_bank][0:1, :], ones_bf[:, :],
                         sc2[:, EB_O + (kc % 4) * TH: EB_O + (kc % 4) * TH + TH],
                         kc == 0, kc == 31,
                         inc=s_t)
                    exp_cons[kc] = s_t.v
                t_ctx = s_t.v
                def frc(v, t_ctx=t_ctx):
                    v.wait_ge(s_t.h, t_ctx)
                    v.reciprocal(row0[0:1, 2 * TH:3 * TH], ps[dn_bank][0:1, :])
                    v.drain()
                vrc = v_run(frc, inc=True)
                p.guard(dn_bank, s_v, vrc)
                p.mm(5, ps[5][:, :], onesrow[0:1, :], row0[0:1, 2 * TH:3 * TH],
                     True, True, inc=s_t, wait=[(s_v, vrc)])
                trb = s_t.v

                def fctx(v, trb=trb, h=h, th=th, ctx_bank=ctx_bank):
                    v.wait_ge(s_t.h, trb)
                    v.tensor_copy(A0[:, 0:TH], ps[5][:, :])
                    v.drain()
                    v.tensor_mul(A1[:, 0:TH], ps[ctx_bank][:, :], A0[:, 0:TH])
                    v.drain()
                    v.tensor_scalar_add(wa[:, CTX_O + h * T + th * TH:
                                           CTX_O + h * T + th * TH + TH],
                                        A1[:, 0:TH], bias["buv"][:, h:h + 1])
                    v.drain()
                vctx = v_run(fctx, inc=True)
                p.guard(5, s_v, vctx)
                p.guard(ctx_bank, s_v, vctx)
            kb_cons[slot] = s_t.v
            vb_cons[slot] = s_t.v
            if h + 2 < NH:
                kb_vals[slot] = load_kb(h + 2)
                vb_vals[slot] = load_vb(h + 2)
            if h == NH - 2:
                load_wo(0)
            if h == NH - 1:
                load_wo(1)
        v_ctx_all = s_v.v

        for piece in range(16):
            slot = piece % 2
            for th in range(2):
                bank = th
                for hcb in range(HC):
                    w = None
                    if hcb == 0:
                        w = [(s_w2[slot], wo_vals[piece])]
                        if piece == 0 and th == 0:
                            w.append((s_v, v_ctx_all))
                    p.mm(bank if hcb == 0 else None, ps[bank][:, :],
                         wa[:, WO_O[slot] + hcb * 128: WO_O[slot] + hcb * 128 + 128],
                         wa[:, CTX_O + hcb * T + th * TH:
                            CTX_O + hcb * T + th * TH + TH],
                         hcb == 0, hcb == HC - 1,
                         inc=(s_t if hcb == HC - 1 else None), wait=w)
                tv = s_t.v
                emit_acc(bank, piece * T + th * TH, tv,
                         bias_col=bias["bo"][:, piece:piece + 1])
            wo_cons[slot] = s_t.v
            if piece + 2 < 16:
                load_wo(piece + 2)
        return s_v.v

    if _STAGE >= 1:
        emit_ffn()
    else:
        def fwait(sy):
            sy.wait_ge(s_in.h, v_pre_in)
        on("sync", fwait)
    p.flush_block()
    if _STAGE >= 2:
        emit_attention()
        p.flush_block()
    if _STAGE >= 3:
        emit_ffn()
        p.flush_block()

    fin_tgt = s_out.bump()

    def fst(sy, fin_tgt=fin_tgt):
        sy.dma_start(out=y_out[:, :].rearrange("(hc p) t -> p hc t", p=128),
                     in_=x_fm[:, :].rearrange("p (hc t) -> p hc t", t=T))\
            .then_inc(s_out.h, 16)
        sy.wait_ge(s_out.h, fin_tgt)
    on("sync", fst)
    p.flush_block()

    ctx.close()
    return nc



def kernel(**inputs):
    shared = _prep_shared(inputs)
    nc = build_program()
    in_maps = []
    for c in range(NCORE):
        m = dict(shared)
        m.update(_prep_core(inputs, c))
        in_maps.append(m)
    res = run_bass_kernel_spmd(nc, in_maps, core_ids=list(range(NCORE)))
    out = np.empty((B, S, HID), np.float32)
    for c in range(NCORE):
        b, part = c // 4, c % 4
        y = res.results[c]["y"]
        out[b, part * T:(part + 1) * T, :] = y.T
    return out


# version 02
# speedup vs baseline: 1.0075x; 1.0075x over previous
import math
import numpy as np
from contextlib import ExitStack

import concourse.bass as bass
import concourse.mybir as mybir
from concourse.bass_utils import run_bass_kernel_spmd

F32 = mybir.dt.float32
BF16 = mybir.dt.bfloat16
AF = mybir.ActivationFunctionType
ALU = mybir.AluOpType
AX = mybir.AxisListType.X

B, S, HID = 2, 4096, 2048
NCORE = 8
T = 1024
TH = 512
HC = 16
E, I, IS = 16, 512, 2048
NH, D, RD, KVC, QC = 16, 128, 64, 512, 1536
HALF = 32
SCALE = float(1.0 / np.sqrt(np.float32(128.0)))
EPS = 1e-5
ROW_B = 32

_STAGE = 3
_FFN_CUT = "full"



def _prep_shared(inp):
    f32 = np.float32
    cc = np.ascontiguousarray
    d = {}

    w1T = np.asarray(inp["L0_ffn_w1"], f32).transpose(0, 2, 1)
    w3T = np.asarray(inp["L0_ffn_w3"], f32).transpose(0, 2, 1)
    pieces = np.empty((E, 4, HID, 256), f32)
    pieces[:, 0] = w1T[:, :, 0:256]
    pieces[:, 1] = w3T[:, :, 0:256]
    pieces[:, 2] = w1T[:, :, 256:512]
    pieces[:, 3] = w3T[:, :, 256:512]
    d["w13"] = cc(pieces)
    d["w2T"] = cc(np.asarray(inp["L0_ffn_w2"], f32).transpose(0, 2, 1))

    wsgT = np.asarray(inp["L0_ffn_ws_gate"], f32).T
    wsuT = np.asarray(inp["L0_ffn_ws_up"], f32).T
    gu = np.empty((16, HID, 256), f32)
    for j in range(8):
        gu[2 * j] = wsgT[:, j * 256:(j + 1) * 256]
        gu[2 * j + 1] = wsuT[:, j * 256:(j + 1) * 256]
    d["wsgu"] = cc(gu)
    wsdT = np.asarray(inp["L0_ffn_ws_down"], f32).T
    d["wsd"] = cc(wsdT.reshape(IS, 8, 256).transpose(1, 0, 2))

    d["gate_wT"] = cc(np.asarray(inp["L0_ffn_gate_w"], f32).T
                      .reshape(16, 128, E).transpose(1, 0, 2).reshape(128, 16 * E))
    d["sgwT"] = cc(np.asarray(inp["L0_ffn_shared_gate_w"], f32).T
                   .reshape(16, 128).T)

    d["wdkv"] = cc(np.asarray(inp["L0_attn_W_DKV"], f32))
    d["wdq"] = cc(np.asarray(inp["L0_attn_W_DQ"], f32))
    d["wuq"] = cc(np.asarray(inp["L0_attn_W_UQ"], f32))
    d["wuk"] = cc(np.asarray(inp["L0_attn_W_UK"], f32))
    d["wuv"] = cc(np.asarray(inp["L0_attn_W_UV"], f32))
    d["wqr"] = cc(np.asarray(inp["L0_attn_W_QR"], f32))
    d["wkr"] = cc(np.asarray(inp["L0_attn_W_KR"], f32))
    d["wo"] = cc(np.asarray(inp["L0_attn_W_O"], f32))

    def chunked(b, n):
        return cc(np.asarray(b, f32).reshape(n, 128).T)
    d["bdkv"] = chunked(inp["L0_attn_b_DKV"], 4)
    d["bdq"] = chunked(inp["L0_attn_b_DQ"], 12)
    d["buq"] = chunked(inp["L0_attn_b_UQ"], 16)
    d["buk"] = chunked(inp["L0_attn_b_UK"], 16)
    d["bo"] = chunked(inp["L0_attn_b_O"], 16)
    d["buv"] = chunked(inp["L0_attn_b_UV"], 16)
    d["ident"] = np.eye(128, dtype=f32)
    return d


def _prep_core(inp, core):
    b, part = core // 4, core % 4
    r0 = part * T
    f32 = np.float32
    d = {}
    x = np.asarray(inp["L0_attn_hidden_states"], f32)
    d["x_fm"] = np.ascontiguousarray(x[b, r0:r0 + T, :].T)

    j = np.arange(0, RD, 2, dtype=f32) / f32(RD)
    inv_freq = (1.0 / (f32(10000.0) ** j)).astype(f32)
    t = np.arange(r0, r0 + T, dtype=f32)
    freqs = (t[None, :] * inv_freq[:, None]).astype(f32)
    d["cosq"] = np.cos(freqs.astype(np.float64)).astype(f32)
    d["sinq"] = np.sin(freqs.astype(np.float64)).astype(f32)
    return d



class Sem:
    def __init__(self, ctx, nc, name, step=1):
        self.h = ctx.enter_context(nc.semaphore(name))
        self.v = 0
        self.step = step

    def bump(self):
        self.v += self.step
        return self.v


class P:

    ENGS = ("gpsimd", "sync", "tensor", "vector", "scalar")

    def __init__(self, nc, ctx):
        self.nc = nc
        self.q = {e: [] for e in self.ENGS}
        self.ps = [ctx.enter_context(nc.psum_tensor(f"ps{i}", [128, TH], F32))
                   for i in range(8)]
        self.bank_guard = [None] * 8

    def on(self, eng, fn):
        self.q[eng].append(fn)

    def flush_block(self):
        with self.nc.Block() as block:
            for name in self.ENGS:
                fns = self.q[name]
                if fns:
                    def body(eng, fns=fns):
                        for f in fns:
                            f(eng)
                    getattr(block, name)(body)
        self.q = {e: [] for e in self.ENGS}

    def mm(self, bank, out_ap, lhs, rhs, start, stop, inc=None, wait=None):
        w = list(wait or [])
        if start and bank is not None and self.bank_guard[bank] is not None:
            w.append(self.bank_guard[bank])
            self.bank_guard[bank] = None
        if inc is not None:
            inc.v += inc.step

        def f(t, out_ap=out_ap, lhs=lhs, rhs=rhs, start=start, stop=stop,
              inc=inc, w=w):
            for sem, val in w:
                t.wait_ge(sem.h, val)
            ins = t.matmul(out_ap, lhs, rhs, start=start, stop=stop)
            if inc is not None:
                ins.then_inc(inc.h, inc.step)
        self.on("tensor", f)

    def transpose(self, bank, out_ap, in_ap, ident_ap, inc=None, wait=None):
        w = list(wait or [])
        if bank is not None and self.bank_guard[bank] is not None:
            w.append(self.bank_guard[bank])
            self.bank_guard[bank] = None
        if inc is not None:
            inc.v += inc.step

        def f(t, out_ap=out_ap, in_ap=in_ap, ident_ap=ident_ap, inc=inc, w=w):
            for sem, val in w:
                t.wait_ge(sem.h, val)
            ins = t.transpose(out_ap, in_ap, ident_ap)
            if inc is not None:
                ins.then_inc(inc.h, inc.step)
        self.on("tensor", f)

    def guard(self, bank, sem, val):
        self.bank_guard[bank] = (sem, val)


def build_program():
    nc = bass.Bass("TRN2", target_bir_lowering=False, dynamic_dma_scratch_size=16384)
    dt = nc.dram_tensor

    x_in = dt("x_fm", [HID, T], F32, kind="ExternalInput")
    y_out = dt("y", [HID, T], F32, kind="ExternalOutput")
    cosq_in = dt("cosq", [HALF, T], F32, kind="ExternalInput")
    sinq_in = dt("sinq", [HALF, T], F32, kind="ExternalInput")
    ident_in = dt("ident", [128, 128], F32, kind="ExternalInput")
    w13_in = dt("w13", [E, 4, HID, 256], F32, kind="ExternalInput")
    w2T_in = dt("w2T", [E, I, HID], F32, kind="ExternalInput")
    wsgu_in = dt("wsgu", [16, HID, 256], F32, kind="ExternalInput")
    wsd_in = dt("wsd", [8, IS, 256], F32, kind="ExternalInput")
    gate_wT_in = dt("gate_wT", [128, 16 * E], F32, kind="ExternalInput")
    sgwT_in = dt("sgwT", [128, 16], F32, kind="ExternalInput")
    wdkv_in = dt("wdkv", [HID, KVC], F32, kind="ExternalInput")
    wdq_in = dt("wdq", [HID, QC], F32, kind="ExternalInput")
    wuq_in = dt("wuq", [QC, HID], F32, kind="ExternalInput")
    wuk_in = dt("wuk", [KVC, HID], F32, kind="ExternalInput")
    wuv_in = dt("wuv", [KVC, HID], F32, kind="ExternalInput")
    wqr_in = dt("wqr", [D, RD], F32, kind="ExternalInput")
    wkr_in = dt("wkr", [D, RD], F32, kind="ExternalInput")
    wo_in = dt("wo", [HID, HID], F32, kind="ExternalInput")
    b_in = {nm: dt(nm, [128, n], F32, kind="ExternalInput")
            for nm, n in [("bdkv", 4), ("bdq", 12), ("buq", 16), ("buk", 16),
                          ("bo", 16), ("buv", 16)]}

    kv_cin = dt("kv_cin", [4096, T], BF16)
    kv_cout = dt("kv_cout", [4 * 4096, T], BF16)

    ctx = ExitStack()
    En = ctx.enter_context

    x_fm = En(nc.sbuf_tensor("x_sb", [128, HC * T], F32))
    xln = En(nc.sbuf_tensor("xln", [128, HC * T], BF16))
    wa = En(nc.sbuf_tensor("warena", [128, 24576], BF16))
    sc2 = En(nc.sbuf_tensor("scratch2", [128, 7168], BF16))
    cq_sb = En(nc.sbuf_tensor("cq_sb", [128, 12 * T], BF16))
    A0 = En(nc.sbuf_tensor("A0", [128, T], F32))
    A1 = En(nc.sbuf_tensor("A1", [128, T], F32))
    B0 = A1[0:16, :]
    b0bf = En(nc.sbuf_tensor("b0bf", [16, T], BF16))
    identb = En(nc.sbuf_tensor("identb", [16, 16], BF16))
    row0 = En(nc.sbuf_tensor("row0", [1, 1536], F32))
    psacc = A0
    cosq = En(nc.sbuf_tensor("cosq_sb", [HALF, T], BF16))
    sinq = En(nc.sbuf_tensor("sinq_sb", [HALF, T], BF16))
    comb_tm = En(nc.sbuf_tensor("comb_tm", [128, 128], F32))
    small = En(nc.sbuf_tensor("small", [128, 64], F32))
    ident = En(nc.sbuf_tensor("ident_sb", [128, 128], F32))
    ones_bf = En(nc.sbuf_tensor("ones_bf", [128, 1], BF16))
    ones_f = En(nc.sbuf_tensor("ones_f", [128, 1], F32))
    onesrow = En(nc.sbuf_tensor("onesrow", [1, 128], F32))
    halfrow = En(nc.sbuf_tensor("halfrow", [1, 128], F32))
    gate_wT = En(nc.sbuf_tensor("gate_wT_sb", [128, HC * E], BF16))
    sgwT = En(nc.sbuf_tensor("sgwT_sb", [128, HC], BF16))
    wqr = En(nc.sbuf_tensor("wqr_sb", [128, RD], BF16))
    wkr = En(nc.sbuf_tensor("wkr_sb", [128, RD], BF16))
    bias = {nm: En(nc.sbuf_tensor("b_" + nm, [128, n], F32))
            for nm, n in [("bdkv", 4), ("bdq", 12), ("buq", 16), ("buk", 16),
                          ("bo", 16), ("buv", 16)]}

    HMID_O, GS_O, U1_O, XSQ_O = 0, 4096, 5120, 6144
    CKV_O, EB_O, QEFF_O, SQQ_O = 0, 0, 2048, 3072
    KEFFO_O, KHEAD_O, QHEAD_O = 4096, 5120, 6144

    RING_OFF = [0, 4096]
    W2_OFF = [8192, 16384]
    MID_O = 8192
    WDKV_O, WDQ_O, WUK_O, WUV_O = 0, 0, 0, 8192
    WUQH_O = [0, 1536]
    CTX_O = 3072
    WO_O = [19456, 21504]

    s_in = Sem(ctx, nc, "dma_in", 16)
    s_out = Sem(ctx, nc, "dma_out", 16)
    s_ring = [Sem(ctx, nc, f"ring{i}", 16) for i in range(2)]
    s_w2 = [Sem(ctx, nc, f"w2_{i}", 16) for i in range(2)]
    s_aw = [Sem(ctx, nc, f"aw{i}", 16) for i in range(4)]
    s_kb = [Sem(ctx, nc, f"kb{i}", 16) for i in range(2)]
    s_vb = [Sem(ctx, nc, f"vb{i}", 16) for i in range(2)]
    s_t = Sem(ctx, nc, "t_main", 1)
    s_v = Sem(ctx, nc, "v_main", 1)
    s_s = Sem(ctx, nc, "s_main", 1)
    s_cc = Sem(ctx, nc, "cc", 1)

    p = P(nc, ctx)
    on = p.on
    ps = p.ps

    def sync_load(dst_ap, src_ap, wait_t=None):
        val = s_in.bump()

        def f(sy, dst_ap=dst_ap, src_ap=src_ap, wait_t=wait_t):
            if wait_t is not None:
                sy.wait_ge(wait_t[0].h, wait_t[1])
            sy.dma_start(out=dst_ap, in_=src_ap).then_inc(s_in.h, 16)
        on("sync", f)
        return val

    def sync_store(dst_ap, src_ap, wait):
        val = s_out.bump()

        def f(sy, dst_ap=dst_ap, src_ap=src_ap, wait=wait):
            for sem, v in wait:
                sy.wait_ge(sem.h, v)
            sy.dma_start(out=dst_ap, in_=src_ap).then_inc(s_out.h, 16)
        on("sync", f)
        return val

    def g_load(sem, dst_ap, src_ap, wait_t=None, slow=False):
        val = sem.bump()

        def f(g, dst_ap=dst_ap, src_ap=src_ap, wait_t=wait_t, sem=sem, slow=slow):
            if wait_t is not None:
                g.wait_ge(wait_t[0].h, wait_t[1])
            g.dma_start(out=dst_ap, in_=src_ap,
                        allow_slow_non_contiguous=slow).then_inc(sem.h, 16)
        on("gpsimd", f)
        return val

    def v_run(fn, *, inc=False):
        if inc:
            val = s_v.bump()

            def f(v, fn=fn):
                fn(v)
                v.drain().then_inc(s_v.h, 1)
            on("vector", f)
            return val
        on("vector", fn)
        return None

    def s_run(fn, *, inc=False):
        if inc:
            val = s_s.bump()

            def f(sc, fn=fn):
                fn(sc)
                sc.drain().then_inc(s_s.h, 1)
            on("scalar", f)
            return val
        on("scalar", fn)
        return None

    acc_state = {"slot": 0, "guard": [None, None]}

    def emit_acc(bank, dst_lo, tv, bias_col=None):
        slot = acc_state["slot"]
        acc_state["slot"] = 1 - slot

        def fad(v, bank=bank, slot=slot, tv=tv, dst_lo=dst_lo, bias_col=bias_col):
            v.wait_ge(s_t.h, tv)
            if bias_col is None:
                v.tensor_add(psacc[:, slot * TH:(slot + 1) * TH],
                             x_fm[:, dst_lo:dst_lo + TH], ps[bank][:, :])
            else:
                v.scalar_tensor_tensor(psacc[:, slot * TH:(slot + 1) * TH],
                                       ps[bank][:, :], bias_col,
                                       x_fm[:, dst_lo:dst_lo + TH],
                                       ALU.add, ALU.add)
            v.drain()
            v.tensor_copy(x_fm[:, dst_lo:dst_lo + TH],
                          psacc[:, slot * TH:(slot + 1) * TH])
            v.drain()
        va = v_run(fad, inc=True)
        p.guard(bank, s_v, va)
        return va

    v_x = sync_load(x_fm[:, :].rearrange("p (hc t) -> p hc t", t=T),
                    x_in[:, :].rearrange("(hc p) t -> p hc t", p=128))
    sync_load(ident[:, :], ident_in[:, :])
    for nm in bias:
        sync_load(bias[nm][:, :], b_in[nm][:, :])
    v_pre_in = s_in.v

    g_load(s_aw[3], identb[:, :], ident_in[0:16, 0:16])
    g_load(s_aw[3], cosq[:, :], cosq_in[:, :])
    g_load(s_aw[3], sinq[:, :], sinq_in[:, :])
    g_load(s_aw[3], gate_wT[:, :], gate_wT_in[:, :])
    g_load(s_aw[3], sgwT[:, :], sgwT_in[:, :])
    g_load(s_aw[3], wqr[:, :], wqr_in[:, :])
    v_smallw = g_load(s_aw[3], wkr[:, :], wkr_in[:, :])

    def f_const(v):
        v.memset(ones_bf[:, :], 1.0)
        v.memset(ones_f[:, :], 1.0)
        v.memset(onesrow[:, :], 1.0)
        v.memset(halfrow[:, :], 0.5)
    v_const = v_run(f_const, inc=True)

    def emit_ln():
        x_ready = [(s_in, v_pre_in), (s_v, s_v.v), (s_v, v_const)]
        t_base = s_t.v
        for th in range(2):
            t0 = th * TH
            sq_cons = {}
            for hcb in range(HC):
                slot = hcb % 2
                slot_free = sq_cons.get(hcb - 2, t_base)

                def fs(sc, hcb=hcb, slot=slot, t0=t0, slot_free=slot_free,
                       first=(hcb == 0 and th == 0)):
                    sc.wait_ge(s_t.h, slot_free)
                    if first:
                        for sem, val in x_ready:
                            sc.wait_ge(sem.h, val)
                    ins = sc.activation(
                        sc2[:, XSQ_O + slot * TH: XSQ_O + (slot + 1) * TH],
                        x_fm[:, hcb * T + t0: hcb * T + t0 + TH], AF.Square)
                    ins.then_inc(s_s.h, 1)
                on("scalar", fs)
                sqv = s_s.bump()
                w = [(s_s, sqv)]
                if hcb == 0:
                    w = w + list(x_ready)
                    if p.bank_guard[4] is not None:
                        w.append(p.bank_guard[4]); p.bank_guard[4] = None
                    if p.bank_guard[5] is not None:
                        w.append(p.bank_guard[5]); p.bank_guard[5] = None
                p.mm(None, ps[4][0:1, :], ones_f[:, :],
                     x_fm[:, hcb * T + t0: hcb * T + t0 + TH],
                     hcb == 0, hcb == HC - 1, wait=(w if hcb == 0 else None))
                p.mm(None, ps[5][0:1, :], ones_bf[:, :],
                     sc2[:, XSQ_O + slot * TH: XSQ_O + (slot + 1) * TH],
                     hcb == 0, hcb == HC - 1,
                     inc=s_t, wait=[(s_s, sqv)])
                sq_cons[hcb] = s_t.v
            t_base = s_t.v
            tv = s_t.v

            def frow(v, tv=tv):
                v.wait_ge(s_t.h, tv)
                v.tensor_scalar_mul(row0[0:1, 0:TH], ps[4][0:1, :], 1.0 / HID)
                v.tensor_scalar_mul(row0[0:1, TH:2 * TH], ps[5][0:1, :], 1.0 / HID)
                v.drain()
                v.tensor_mul(row0[0:1, 2 * TH:3 * TH], row0[0:1, 0:TH],
                             row0[0:1, 0:TH])
                v.drain()
                v.tensor_sub(row0[0:1, TH:2 * TH], row0[0:1, TH:2 * TH],
                             row0[0:1, 2 * TH:3 * TH])
                v.drain()
                v.tensor_scalar_add(row0[0:1, TH:2 * TH], row0[0:1, TH:2 * TH],
                                    EPS)
                v.drain()
                v.reciprocal(row0[0:1, 2 * TH:3 * TH], row0[0:1, TH:2 * TH])
            v_rowa = v_run(frow, inc=True)
            p.guard(4, s_v, v_rowa)
            p.guard(5, s_v, v_rowa)

            def fsq(sc, v_rowa=v_rowa):
                sc.wait_ge(s_v.h, v_rowa)
                sc.activation(row0[0:1, TH:2 * TH], row0[0:1, 2 * TH:3 * TH],
                              AF.Sqrt)
            sv = s_run(fsq, inc=True)

            def fa(v, sv=sv):
                v.wait_ge(s_s.h, sv)
                v.tensor_mul(row0[0:1, 2 * TH:3 * TH], row0[0:1, 0:TH],
                             row0[0:1, TH:2 * TH])
            v_a = v_run(fa, inc=True)
            p.mm(6, ps[6][:, :], onesrow[0:1, :], row0[0:1, TH:2 * TH],
                 True, True, wait=[(s_v, v_a)])
            p.mm(7, ps[7][:, :], onesrow[0:1, :], row0[0:1, 2 * TH:3 * TH],
                 True, True, inc=s_t)
            tv2 = s_t.v

            def fbc(v, tv2=tv2):
                v.wait_ge(s_t.h, tv2)
                v.tensor_copy(A0[:, 0:TH], ps[6][:, :])
                v.tensor_copy(A1[:, 0:TH], ps[7][:, :])
            v_bc = v_run(fbc, inc=True)
            p.guard(6, s_v, v_bc)
            p.guard(7, s_v, v_bc)

            for hcb in range(HC):
                def fap(v, hcb=hcb, t0=t0):
                    sl = slice(hcb * T + t0, hcb * T + t0 + TH)
                    v.tensor_mul(A1[:, TH:2 * TH], x_fm[:, sl], A0[:, 0:TH])
                    v.drain()
                    v.tensor_sub(xln[:, sl], A1[:, TH:2 * TH], A1[:, 0:TH])
                    v.drain()
                on("vector", fap)

            def fd(v):
                pass
            last = v_run(lambda v: None, inc=True)
        return last

    def emit_router(v_xln):
        for th in range(2):
            bank = th
            for hcb in range(HC):
                w = None
                if hcb == 0:
                    w = [(s_v, v_xln), (s_aw[3], v_smallw)]
                p.mm(bank if hcb == 0 else None, ps[bank][0:16, :],
                     gate_wT[:, hcb * E:(hcb + 1) * E],
                     xln[:, hcb * T + th * TH: hcb * T + th * TH + TH],
                     hcb == 0, hcb == HC - 1,
                     inc=(s_t if hcb == HC - 1 else None), wait=w)
            tv = s_t.v

            def fe(v, tv=tv, th=th, bank=bank):
                v.wait_ge(s_t.h, tv)
                v.tensor_copy(B0[0:16, th * TH:(th + 1) * TH], ps[bank][0:16, :])
            ve = v_run(fe, inc=True)
            p.guard(bank, s_v, ve)
        v_lg = s_v.v
        for jb in range(8):
            p.transpose(2 if jb == 0 else None,
                        ps[2][:, jb * 16:(jb + 1) * 16],
                        B0[0:16, jb * 128:(jb + 1) * 128], ident[0:16, 0:16],
                        inc=(s_t if jb == 7 else None),
                        wait=([(s_v, v_lg)] if jb == 0 else None))
        tv_tr = s_t.v

        def fmx(v, tv_tr=tv_tr):
            v.wait_ge(s_t.h, tv_tr)
            lg3 = ps[2][:, 0:128].rearrange("p (g e) -> p g e", e=16)
            v.reduce_max(small[:, 0:8], lg3, axis=AX)
            v.drain()
            v.tensor_scalar_mul(small[:, 8:16], small[:, 0:8], -1.0)
        vmx = v_run(fmx, inc=True)
        for g8 in range(8):
            def fexp(sc, g8=g8, vmx=vmx):
                if g8 == 0:
                    sc.wait_ge(s_v.h, vmx)
                sc.activation(comb_tm[:, g8 * 16:(g8 + 1) * 16],
                              ps[2][:, g8 * 16:(g8 + 1) * 16], AF.Exp,
                              bias=small[:, 8 + g8:9 + g8], scale=1.0)
            on("scalar", fexp)
        sv = s_run(lambda sc: None, inc=True)
        p.guard(2, s_s, sv)

        def ftop(v, sv=sv):
            v.wait_ge(s_s.h, sv)
            rt3 = comb_tm[:, 0:128].rearrange("p (g e) -> p g e", e=16)
            v.reduce_max(small[:, 0:8], rt3, axis=AX)
            v.drain()
            for g8 in range(8):
                v.tensor_scalar(A0[:, g8 * 16:(g8 + 1) * 16],
                                comb_tm[:, g8 * 16:(g8 + 1) * 16],
                                small[:, g8:g8 + 1], None, ALU.is_lt)
            v.drain()
            v.tensor_mul(A0[:, 0:128], comb_tm[:, 0:128], A0[:, 0:128])
            v.drain()
            v.reduce_max(small[:, 8:16],
                         A0[:, 0:128].rearrange("p (g e) -> p g e", e=16), axis=AX)
            v.drain()
            for g8 in range(8):
                v.tensor_scalar(A0[:, g8 * 16:(g8 + 1) * 16],
                                comb_tm[:, g8 * 16:(g8 + 1) * 16],
                                small[:, 8 + g8:9 + g8], None, ALU.is_ge)
            v.drain()
            v.tensor_mul(A0[:, 0:128], comb_tm[:, 0:128], A0[:, 0:128])
            v.tensor_add(small[:, 0:8], small[:, 0:8], small[:, 8:16])
            v.drain()
            v.reciprocal(small[:, 0:8], small[:, 0:8])
            v.drain()
            for g8 in range(8):
                v.tensor_scalar_mul(comb_tm[:, g8 * 16:(g8 + 1) * 16],
                                    A0[:, g8 * 16:(g8 + 1) * 16],
                                    small[:, g8:g8 + 1])
        vtop = v_run(ftop, inc=True)
        for jb in range(8):
            bank = 3 if jb < 4 else 4
            col = (jb % 4) * 128
            p.transpose(bank if (jb % 4 == 0) else None,
                        ps[bank][0:16, col:col + 128],
                        comb_tm[:, jb * 16:(jb + 1) * 16], ident[:, :],
                        inc=(s_t if jb == 7 else None),
                        wait=([(s_v, vtop)] if jb == 0 else None))
        tv2 = s_t.v

        def fcfm(v, tv2=tv2):
            v.wait_ge(s_t.h, tv2)
            v.tensor_copy(b0bf[0:16, 0:TH], ps[3][0:16, 0:TH])
            v.tensor_copy(b0bf[0:16, TH:T], ps[4][0:16, 0:TH])
        vfm = v_run(fcfm, inc=True)
        p.guard(3, s_v, vfm)
        p.guard(4, s_v, vfm)
        return vfm

    def emit_ffn():
        v_xln = emit_ln()
        if _FFN_CUT == "ln":
            return s_v.v
        v_comb = emit_router(v_xln)
        if _FFN_CUT == "router":
            return s_v.v

        ring_cons = [None, None]
        ring_fill = [0, 0]

        def ring_load(slot, src_ap):
            wt = None
            if ring_cons[slot] is not None:
                wt = (s_t, ring_cons[slot])
            val = g_load(s_ring[slot],
                         wa[:, RING_OFF[slot]:RING_OFF[slot] + 4096]
                         .rearrange("p (hc i) -> p hc i", i=256),
                         src_ap, wait_t=wt)
            ring_fill[slot] += 1
            return val

        ring_srcs = []
        for e in range(E):
            for pc in range(4):
                ring_srcs.append(w13_in[e, pc].rearrange("(hc p) i -> p hc i", p=128))
        for half in range(2):
            for pc in range(8):
                ring_srcs.append(wsgu_in[half * 8 + pc]
                                 .rearrange("(hc p) i -> p hc i", p=128))
            for dp in range(8):
                ring_srcs.append(wsd_in[dp].rearrange("(ic p) h -> p ic h", p=128))
        ring_vals = {}
        ridx = 0

        def ring_issue():
            nonlocal ridx
            slot = ridx % 2
            ring_vals[ridx] = (slot, ring_load(slot, ring_srcs[ridx]))
            ridx += 1

        if _FFN_CUT != "e1nodma":
            ring_issue()
            ring_issue()

        w2_cons = [None, None]

        def w2_load(bufi, e):
            wt = (s_t, w2_cons[bufi]) if w2_cons[bufi] is not None else None
            return g_load(s_w2[bufi],
                          wa[:, W2_OFF[bufi]:W2_OFF[bufi] + 4 * HID]
                          .rearrange("p (ic h) -> p ic h", h=HID),
                          w2T_in[e].rearrange("(ic p) h -> p ic h", p=128),
                          wait_t=wt)

        w2_vals = [w2_load(0, 0) if _FFN_CUT != "e1nodma" else None, None]
        gs_guard = [None, None]

        if _FFN_CUT == "e1nocb":
            return s_v.v
        n_exp = 1 if _FFN_CUT.startswith("e1") else E
        if _FFN_CUT == "e1x":
            n_exp = 1
        for e in range(n_exp):
            p.mm(5, ps[5][0:1, :], identb[0:16, e:e + 1], b0bf[0:16, 0:TH],
                 True, True,
                 wait=([(s_v, v_comb), (s_aw[3], v_smallw)] if e == 0 else None))
            p.mm(6, ps[6][0:1, :], identb[0:16, e:e + 1], b0bf[0:16, TH:T],
                 True, True, inc=s_t)
            tsel = s_t.v

            def fsel(sc, tsel=tsel):
                sc.wait_ge(s_t.h, tsel)
                sc.activation(row0[0:1, 0:TH], ps[5][0:1, :], AF.Copy)
                sc.activation(row0[0:1, TH:2 * TH], ps[6][0:1, :], AF.Copy)
            ssel = s_run(fsel, inc=True)
            p.guard(5, s_s, ssel)
            p.guard(6, s_s, ssel)
            p.mm(5, ps[5][:, :], halfrow[0:1, :], row0[0:1, 0:TH], True, True,
                 wait=[(s_s, ssel)])
            p.mm(6, ps[6][:, :], halfrow[0:1, :], row0[0:1, TH:2 * TH],
                 True, True, inc=s_t)
            tcb = s_t.v

            def fcb(v, tcb=tcb):
                v.wait_ge(s_t.h, tcb)
                v.tensor_copy(A0[:, 0:TH], ps[5][:, :])
                v.tensor_copy(A0[:, TH:T], ps[6][:, :])
            v_cb = v_run(fcb, inc=True)
            p.guard(5, s_v, v_cb)
            p.guard(6, s_v, v_cb)
            if _FFN_CUT in ("e1a", "e1nodma"):
                return s_v.v

            for ic in range(4):
                pci = (0 if ic < 2 else 2)
                coloff = (ic % 2) * 128
                r1 = ring_vals[e * 4 + pci]
                r3 = ring_vals[e * 4 + pci + 1]
                for th in range(2):
                    bg = ic % 2
                    bu = 2 + ic % 2
                    for hcb in range(HC):
                        w = None
                        if hcb == 0:
                            w = [(s_ring[r1[0]], r1[1])]
                            if ic == 0 and th == 0:
                                w.append((s_v, v_xln))
                        p.mm(bg if hcb == 0 else None, ps[bg][:, :],
                             wa[:, RING_OFF[r1[0]] + hcb * 256 + coloff:
                                RING_OFF[r1[0]] + hcb * 256 + coloff + 128],
                             xln[:, hcb * T + th * TH: hcb * T + th * TH + TH],
                             hcb == 0, hcb == HC - 1,
                             inc=(s_t if hcb == HC - 1 else None), wait=w)
                    tg = s_t.v

                    def fsilu(sc, tg=tg, bg=bg, th=th, gfree=gs_guard[th]):
                        sc.wait_ge(s_t.h, tg)
                        if gfree is not None:
                            sc.wait_ge(s_v.h, gfree)
                        sc.activation(sc2[:, GS_O + th * TH: GS_O + (th + 1) * TH],
                                      ps[bg][:, :], AF.Silu)
                    sgv = s_run(fsilu, inc=True)
                    p.guard(bg, s_s, sgv)
                    if _FFN_CUT == "e1c":
                        gs_guard[th] = None
                        continue
                    for hcb in range(HC):
                        w = [(s_ring[r3[0]], r3[1])] if hcb == 0 else None
                        p.mm(bu if hcb == 0 else None, ps[bu][:, :],
                             wa[:, RING_OFF[r3[0]] + hcb * 256 + coloff:
                                RING_OFF[r3[0]] + hcb * 256 + coloff + 128],
                             xln[:, hcb * T + th * TH: hcb * T + th * TH + TH],
                             hcb == 0, hcb == HC - 1,
                             inc=(s_t if hcb == HC - 1 else None), wait=w)
                    tu = s_t.v

                    def fmid(v, tu=tu, sgv=sgv, bu=bu, ic=ic, th=th, v_cb=v_cb):
                        v.wait_ge(s_t.h, tu)
                        v.wait_ge(s_s.h, sgv)
                        v.wait_ge(s_v.h, v_cb)
                        v.tensor_mul(sc2[:, U1_O: U1_O + TH],
                                     ps[bu][:, :], A0[:, th * TH:(th + 1) * TH])
                        v.drain()
                        v.tensor_mul(
                            sc2[:, HMID_O + ic * T + th * TH:
                                HMID_O + ic * T + th * TH + TH],
                            sc2[:, U1_O: U1_O + TH],
                            sc2[:, GS_O + th * TH: GS_O + (th + 1) * TH])
                        v.drain()
                    vmid = v_run(fmid, inc=True)
                    p.guard(bu, s_v, vmid)
                    gs_guard[th] = vmid
                if ic in (1, 3):
                    ring_cons[r1[0]] = s_t.v
                    ring_cons[r3[0]] = s_t.v
                    if ridx < len(ring_srcs):
                        ring_issue()
                    if ridx < len(ring_srcs):
                        ring_issue()
            v_hmid = s_v.v
            if _FFN_CUT in ("e1b", "e1c"):
                return s_v.v

            bufi = e % 2
            w2v = w2_vals[bufi]
            if e + 1 < n_exp:
                w2_vals[1 - bufi] = w2_load(1 - bufi, e + 1)
            last_w2_mm = None
            for hcb in range(HC):
                for th in range(2):
                    bm = 4 + (hcb * 2 + th) % 4
                    for ic in range(4):
                        w = None
                        if hcb == 0 and th == 0 and ic == 0:
                            w = [(s_w2[bufi], w2v), (s_v, v_hmid)]
                        p.mm(bm if ic == 0 else None, ps[bm][:, :],
                             wa[:, W2_OFF[bufi] + ic * HID + hcb * 128:
                                W2_OFF[bufi] + ic * HID + hcb * 128 + 128],
                             sc2[:, HMID_O + ic * T + th * TH:
                                 HMID_O + ic * T + th * TH + TH],
                             ic == 0, ic == 3,
                             inc=(s_t if ic == 3 else None), wait=w)
                    tv = s_t.v
                    if _FFN_CUT == "e1k1":
                        if hcb == 0 and th == 0:
                            emit_acc(bm, hcb * T + th * TH, tv)
                        else:
                            def fdisc(v, tv=tv, bm=bm):
                                v.wait_ge(s_t.h, tv)
                                v.tensor_copy(psacc[:, 0:TH], ps[bm][:, :])
                                v.drain()
                            vd = v_run(fdisc, inc=True)
                            p.guard(bm, s_v, vd)
                    elif _FFN_CUT != "e1m":
                        emit_acc(bm, hcb * T + th * TH, tv)
                    else:
                        def fdisc(v, tv=tv, bm=bm):
                            v.wait_ge(s_t.h, tv)
                            v.tensor_copy(psacc[:, 0:TH], ps[bm][:, :])
                            v.drain()
                        vd = v_run(fdisc, inc=True)
                        p.guard(bm, s_v, vd)
            w2_cons[bufi] = s_t.v
        if _FFN_CUT.startswith("e1") or _FFN_CUT == "e16":
            return s_v.v

        for th in range(2):
            bank = th
            for hcb in range(HC):
                p.mm(bank if hcb == 0 else None, ps[bank][0:1, :],
                     sgwT[:, hcb:hcb + 1],
                     xln[:, hcb * T + th * TH: hcb * T + th * TH + TH],
                     hcb == 0, hcb == HC - 1,
                     inc=(s_t if hcb == HC - 1 else None))
            tv = s_t.v

            def fsig(sc, tv=tv, th=th, bank=bank):
                sc.wait_ge(s_t.h, tv)
                sc.activation(row0[0:1, th * TH:(th + 1) * TH],
                              ps[bank][0:1, :], AF.Sigmoid)
            ssg = s_run(fsig, inc=True)
            p.guard(bank, s_s, ssg)
        sv_gate = s_s.v
        p.mm(5, ps[5][:, :], halfrow[0:1, :], row0[0:1, 0:TH], True, True,
             wait=[(s_s, sv_gate)])
        p.mm(6, ps[6][:, :], halfrow[0:1, :], row0[0:1, TH:T], True, True,
             inc=s_t)
        tgb = s_t.v

        def fgbc(v, tgb=tgb):
            v.wait_ge(s_t.h, tgb)
            v.tensor_copy(A1[:, 0:TH], ps[5][:, :])
            v.tensor_copy(A1[:, TH:T], ps[6][:, :])
        v_gbc = v_run(fgbc, inc=True)
        p.guard(5, s_v, v_gbc)
        p.guard(6, s_v, v_gbc)

        for half in range(2):
            for isc8 in range(8):
                isc = half * 8 + isc8
                gidx = 64 + half * 16 + 2 * (isc8 // 2)
                uidx = gidx + 1
                coloff = (isc % 2) * 128
                if isc % 2 == 0:
                    while ridx <= uidx:
                        ring_issue()
                rg = ring_vals[gidx]
                ru = ring_vals[uidx]
                for th in range(2):
                    bg = isc8 % 2
                    bu = 2 + isc8 % 2
                    for hcb in range(HC):
                        w = None
                        if hcb == 0:
                            w = [(s_ring[rg[0]], rg[1])]
                            if half == 0 and isc8 == 0 and th == 0:
                                if w2_cons[0] is not None:
                                    w.append((s_t, w2_cons[0]))
                        p.mm(bg if hcb == 0 else None, ps[bg][:, :],
                             wa[:, RING_OFF[rg[0]] + hcb * 256 + coloff:
                                RING_OFF[rg[0]] + hcb * 256 + coloff + 128],
                             xln[:, hcb * T + th * TH: hcb * T + th * TH + TH],
                             hcb == 0, hcb == HC - 1,
                             inc=(s_t if hcb == HC - 1 else None), wait=w)
                    tg = s_t.v

                    def fsilu(sc, tg=tg, bg=bg, th=th, gfree=gs_guard[th]):
                        sc.wait_ge(s_t.h, tg)
                        if gfree is not None:
                            sc.wait_ge(s_v.h, gfree)
                        sc.activation(sc2[:, GS_O + th * TH: GS_O + (th + 1) * TH],
                                      ps[bg][:, :], AF.Silu)
                    sgv = s_run(fsilu, inc=True)
                    p.guard(bg, s_s, sgv)
                    for hcb in range(HC):
                        w = [(s_ring[ru[0]], ru[1])] if hcb == 0 else None
                        p.mm(bu if hcb == 0 else None, ps[bu][:, :],
                             wa[:, RING_OFF[ru[0]] + hcb * 256 + coloff:
                                RING_OFF[ru[0]] + hcb * 256 + coloff + 128],
                             xln[:, hcb * T + th * TH: hcb * T + th * TH + TH],
                             hcb == 0, hcb == HC - 1,
                             inc=(s_t if hcb == HC - 1 else None), wait=w)
                    tu = s_t.v

                    def fmid(v, tu=tu, sgv=sgv, bu=bu, isc8=isc8, th=th,
                             v_gbc=v_gbc):
                        v.wait_ge(s_t.h, tu)
                        v.wait_ge(s_s.h, sgv)
                        v.wait_ge(s_v.h, v_gbc)
                        v.tensor_mul(sc2[:, U1_O: U1_O + TH], ps[bu][:, :],
                                     A1[:, th * TH:(th + 1) * TH])
                        v.drain()
                        v.tensor_mul(
                            wa[:, MID_O + isc8 * T + th * TH:
                               MID_O + isc8 * T + th * TH + TH],
                            sc2[:, U1_O: U1_O + TH],
                            sc2[:, GS_O + th * TH: GS_O + (th + 1) * TH])
                        v.drain()
                    vmid = v_run(fmid, inc=True)
                    p.guard(bu, s_v, vmid)
                    gs_guard[th] = vmid
                if isc % 2 == 1:
                    ring_cons[rg[0]] = s_t.v
                    ring_cons[ru[0]] = s_t.v
            v_mid = s_v.v
            for dp in range(8):
                dn_idx = 64 + half * 16 + 8 + dp
                while ridx <= dn_idx:
                    ring_issue()
                rd_ = ring_vals[dn_idx]
                for hh in range(2):
                    hcb = dp * 2 + hh
                    for th in range(2):
                        bm = 4 + (hcb * 2 + th) % 4
                        for i8 in range(8):
                            isc = half * 8 + i8
                            w = None
                            if hh == 0 and th == 0 and i8 == 0:
                                w = [(s_ring[rd_[0]], rd_[1]), (s_v, v_mid)]
                            p.mm(bm if i8 == 0 else None, ps[bm][:, :],
                                 wa[:, RING_OFF[rd_[0]] + isc * 256 + hh * 128:
                                    RING_OFF[rd_[0]] + isc * 256 + hh * 128 + 128],
                                 wa[:, MID_O + i8 * T + th * TH:
                                    MID_O + i8 * T + th * TH + TH],
                                 i8 == 0, i8 == 7,
                                 inc=(s_t if i8 == 7 else None), wait=w)
                        tv = s_t.v
                        emit_acc(bm, hcb * T + th * TH, tv)
                ring_cons[rd_[0]] = s_t.v
            if half == 0:
                pass
        return s_v.v

    def emit_attention():
        v_xln = emit_ln()

        v_wdkv = g_load(s_aw[0], wa[:, WDKV_O:WDKV_O + 8192]
                        .rearrange("p (hc k) -> p hc k", k=KVC),
                        wdkv_in[:, :].rearrange("(hc p) k -> p hc k", p=128))
        for kvcc in range(4):
            for th in range(2):
                bank = (kvcc * 2 + th) % 4
                for hcb in range(HC):
                    w = None
                    if hcb == 0:
                        w = [(s_aw[0], v_wdkv)]
                        if kvcc == 0 and th == 0:
                            w.append((s_v, v_xln))
                    p.mm(bank if hcb == 0 else None, ps[bank][:, :],
                         wa[:, WDKV_O + hcb * KVC + kvcc * 128:
                            WDKV_O + hcb * KVC + kvcc * 128 + 128],
                         xln[:, hcb * T + th * TH: hcb * T + th * TH + TH],
                         hcb == 0, hcb == HC - 1,
                         inc=(s_t if hcb == HC - 1 else None), wait=w)
                tv = s_t.v

                def fevac(sc, tv=tv, kvcc=kvcc, th=th, bank=bank):
                    sc.wait_ge(s_t.h, tv)
                    sc.activation(sc2[:, CKV_O + kvcc * T + th * TH:
                                      CKV_O + kvcc * T + th * TH + TH],
                                  ps[bank][:, :], AF.Identity,
                                  bias=bias["bdkv"][:, kvcc:kvcc + 1], scale=1.0)
                sev = s_run(fevac, inc=True)
                p.guard(bank, s_s, sev)
        v_ckv_t = s_t.v
        v_ckv = s_s.v

        v_wdq = g_load(s_aw[1], wa[:, 0:24576].rearrange("p (hc q) -> p hc q", q=QC),
                       wdq_in[:, :].rearrange("(hc p) q -> p hc q", p=128),
                       wait_t=(s_t, v_ckv_t))
        for qcc in range(12):
            for th in range(2):
                bank = (qcc * 2 + th) % 4
                for hcb in range(HC):
                    w = [(s_aw[1], v_wdq)] if hcb == 0 else None
                    p.mm(bank if hcb == 0 else None, ps[bank][:, :],
                         wa[:, hcb * QC + qcc * 128: hcb * QC + qcc * 128 + 128],
                         xln[:, hcb * T + th * TH: hcb * T + th * TH + TH],
                         hcb == 0, hcb == HC - 1,
                         inc=(s_t if hcb == HC - 1 else None), wait=w)
                tv = s_t.v

                def fevac(sc, tv=tv, qcc=qcc, th=th, bank=bank):
                    sc.wait_ge(s_t.h, tv)
                    sc.activation(cq_sb[:, qcc * T + th * TH:
                                        qcc * T + th * TH + TH],
                                  ps[bank][:, :], AF.Identity,
                                  bias=bias["bdq"][:, qcc:qcc + 1], scale=1.0)
                sev = s_run(fevac, inc=True)
                p.guard(bank, s_s, sev)
        v_cq_t = s_t.v

        v_wuk = g_load(s_aw[2], wa[:, WUK_O:WUK_O + 8192]
                       .rearrange("p (kc h) -> p kc h", h=HID),
                       wuk_in[:, :].rearrange("(kc p) h -> p kc h", p=128),
                       wait_t=(s_t, v_cq_t))
        v_wuv = g_load(s_aw[0], wa[:, WUV_O:WUV_O + 8192]
                       .rearrange("p (kc h) -> p kc h", h=HID),
                       wuv_in[:, :].rearrange("(kc p) h -> p kc h", p=128))

        v_store_vals = []
        for tc in range(8):
            for dts in range(4):
                bank = (tc * 4 + dts) % 4
                for kvcc in range(4):
                    w = None
                    if tc == 0 and dts == 0 and kvcc == 0:
                        w = [(s_aw[0], v_wuv), (s_s, v_ckv), (s_t, v_cq_t)]
                    elif kvcc == 0:
                        w = []
                    p.mm(bank if kvcc == 0 else None, ps[bank][:, :],
                         sc2[:, CKV_O + kvcc * T + tc * 128:
                             CKV_O + kvcc * T + tc * 128 + 128],
                         wa[:, WUV_O + kvcc * HID + dts * TH:
                            WUV_O + kvcc * HID + dts * TH + TH],
                         kvcc == 0, kvcc == 3,
                         inc=(s_t if kvcc == 3 else None), wait=w)
                tv = s_t.v

                def fvadd(sc, tv=tv, tc=tc, dts=dts, bank=bank):
                    sc.wait_ge(s_t.h, tv)
                    sc.activation(xln[:, tc * HID + dts * TH:
                                      tc * HID + dts * TH + TH],
                                  ps[bank][:, :], AF.Copy)
                vv = s_run(fvadd, inc=True)
                p.guard(bank, s_s, vv)
            v_store_vals.append(sync_store(
                kv_cin[2048 + tc * 256: 2048 + (tc + 1) * 256, :]
                .rearrange("(p dh) c -> p dh c", p=128),
                xln[:, tc * HID:(tc + 1) * HID]
                .rearrange("p (dh c) -> p dh c", c=T),
                wait=[(s_s, s_s.v)]))

        def fkz(v):
            v.memset(sc2[ROW_B:64, KEFFO_O:KEFFO_O + T], 0.0)
            v.drain()
            v.memset(sc2[ROW_B:ROW_B + 1, KEFFO_O:KEFFO_O + T], -1.0)
            v.drain()
        v_kz = v_run(fkz, inc=True)

        prev_keff_store = None
        for h in range(NH):
            for th in range(2):
                bank = th
                for kvcc in range(4):
                    w = None
                    if kvcc == 0:
                        w = [(s_aw[2], v_wuk), (s_s, v_ckv)]
                    p.mm(bank if kvcc == 0 else None, ps[bank][:, :],
                         wa[:, WUK_O + kvcc * HID + h * 128:
                            WUK_O + kvcc * HID + h * 128 + 128],
                         sc2[:, CKV_O + kvcc * T + th * TH:
                             CKV_O + kvcc * T + th * TH + TH],
                         kvcc == 0, kvcc == 3,
                         inc=(s_t if kvcc == 3 else None), wait=w)
                tv = s_t.v

                def fke(sc, tv=tv, th=th, bank=bank, h=h, pks=prev_keff_store):
                    sc.wait_ge(s_t.h, tv)
                    if th == 0 and pks is not None:
                        sc.wait_ge(s_out.h, pks)
                    sc.activation(sc2[:, KHEAD_O + th * TH: KHEAD_O + th * TH + TH],
                                  ps[bank][:, :], AF.Identity,
                                  bias=bias["buk"][:, h:h + 1], scale=1.0)
                    sc.activation(sc2[64:128, KEFFO_O + th * TH:
                                      KEFFO_O + th * TH + TH],
                                  ps[bank][64:128, :], AF.Identity,
                                  bias=bias["buk"][64:128, h:h + 1], scale=1.0)
                skv = s_run(fke, inc=True)
                p.guard(bank, s_s, skv)
            v_khead = s_s.v
            for th in range(2):
                p.mm(2, ps[2][0:HALF, :], wkr[:, 0:HALF],
                     sc2[:, KHEAD_O + th * TH: KHEAD_O + th * TH + TH],
                     True, True, wait=[(s_s, v_khead), (s_aw[3], v_smallw)])
                p.mm(3, ps[3][0:HALF, :], wkr[:, HALF:RD],
                     sc2[:, KHEAD_O + th * TH: KHEAD_O + th * TH + TH],
                     True, True, inc=s_t)
                tv = s_t.v

                def frope(v, tv=tv, th=th, h=h, v_kz=v_kz):
                    v.wait_ge(s_t.h, tv)
                    if h == 0 and th == 0:
                        v.wait_ge(s_v.h, v_kz)
                    v.tensor_mul(A0[0:HALF, 0:TH], ps[2][0:HALF, :],
                                 cosq[:, th * TH:(th + 1) * TH])
                    v.tensor_mul(A1[0:HALF, 0:TH], ps[3][0:HALF, :],
                                 sinq[:, th * TH:(th + 1) * TH])
                    v.drain()
                    v.tensor_sub(sc2[0:HALF, KEFFO_O + th * TH:
                                     KEFFO_O + th * TH + TH],
                                 A0[0:HALF, 0:TH], A1[0:HALF, 0:TH])
                    v.drain()
                vr = v_run(frope, inc=True)
                p.guard(2, s_v, vr)
                p.guard(3, s_v, vr)
            prev_keff_store = sync_store(kv_cin[h * 128:(h + 1) * 128, :],
                                         sc2[:, KEFFO_O:KEFFO_O + T],
                                         wait=[(s_v, s_v.v)])
        v_all_stores = s_out.v

        def fcc(g):
            g.wait_ge(s_out.h, v_all_stores)
            g.collective_compute(
                "AllGather", ALU.bypass,
                replica_groups=[[0, 1, 2, 3], [4, 5, 6, 7]],
                ins=[kv_cin.ap().opt()],
                outs=[kv_cout.ap().opt()],
            ).then_inc(s_cc.h, 1)
        on("gpsimd", fcc)
        v_cc = s_cc.bump()

        KB_O = [0, 4096]
        VB_O = [8192, 12288]
        kb_cons = [None, None]
        vb_cons = [None, None]
        wuqh_cons = [None, None]
        wuqh_vals = [None, None]

        def load_kb(h):
            slot = h % 2
            first_w = [(s_cc, v_cc)]
            if kb_cons[slot] is not None:
                first_w.append((s_t, kb_cons[slot]))
            vals = []
            for r in range(4):
                val = s_kb[slot].bump()

                def f(sy, r=r, h=h, slot=slot, w=(first_w if r == 0 else [])):
                    for sem, vv in w:
                        sy.wait_ge(sem.h, vv)
                    sy.dma_start(
                        out=xln[:, KB_O[slot] + r * T: KB_O[slot] + (r + 1) * T],
                        in_=kv_cout[r * 4096 + h * 128: r * 4096 + (h + 1) * 128, :])\
                        .then_inc(s_kb[slot].h, 16)
                on("sync", f)
                vals.append(val)
            return vals[-1]

        def load_vb(h):
            slot = h % 2
            first_w = [(s_cc, v_cc)]
            if vb_cons[slot] is not None:
                first_w.append((s_t, vb_cons[slot]))
            vals = []
            dh = h // 8
            hc8 = (h % 8) * 128
            for r in range(4):
                val = s_vb[slot].bump()

                def f(sy, r=r, slot=slot, dh=dh, hc8=hc8,
                      w=(first_w if r == 0 else [])):
                    for sem, vv in w:
                        sy.wait_ge(sem.h, vv)
                    src = kv_cout[r * 4096 + 2048: r * 4096 + 4096, :]\
                        .rearrange("(tc p dh) c -> p tc dh c", p=128, dh=2)
                    sy.dma_start(
                        out=xln[:, VB_O[slot] + r * T: VB_O[slot] + (r + 1) * T]
                        .rearrange("p (tc o d) -> p tc o d", o=1, d=128),
                        in_=src[:, :, dh:dh + 1, hc8:hc8 + 128])\
                        .then_inc(s_vb[slot].h, 16)
                on("sync", f)
                vals.append(val)
            return vals[-1]

        def load_wuqh(h):
            slot = h % 2
            wt = (s_t, wuqh_cons[slot]) if wuqh_cons[slot] is not None else None
            return g_load(s_aw[slot],
                          wa[:, WUQH_O[slot]:WUQH_O[slot] + 1536]
                          .rearrange("p (qc d) -> p qc d", d=128),
                          wuq_in[:, h * 128:(h + 1) * 128]
                          .rearrange("(qc p) d -> p qc d", p=128), wait_t=wt)

        kb_vals = [load_kb(0), None]
        vb_vals = [load_vb(0), None]
        wuqh_vals = [load_wuqh(0), None]
        kb_vals[1] = load_kb(1)
        vb_vals[1] = load_vb(1)
        wuqh_vals[1] = load_wuqh(1)

        wo_cons = [None, None]
        wo_vals = {}

        def load_wo(piece):
            slot = piece % 2
            wt = (s_t, wo_cons[slot]) if wo_cons[slot] is not None else None
            wo_vals[piece] = g_load(
                s_w2[slot],
                wa[:, WO_O[slot]:WO_O[slot] + 2048]
                .rearrange("p (hc c) -> p hc c", c=128),
                wo_in[:, piece * 128:(piece + 1) * 128]
                .rearrange("(hc p) c -> p hc c", p=128), wait_t=wt)

        for h in range(NH):
            slot = h % 2
            kbv, vbv, wqv = kb_vals[slot], vb_vals[slot], wuqh_vals[slot]
            kb = xln[:, KB_O[slot]:KB_O[slot] + 4096]
            vbuf = xln[:, VB_O[slot]:VB_O[slot] + 4096]
            for th in range(2):
                bank = th
                for qcc in range(12):
                    w = None
                    if qcc == 0:
                        w = [(s_aw[slot], wqv)]
                    p.mm(bank if qcc == 0 else None, ps[bank][:, :],
                         wa[:, WUQH_O[slot] + qcc * 128:
                            WUQH_O[slot] + qcc * 128 + 128],
                         cq_sb[:, qcc * T + th * TH: qcc * T + th * TH + TH],
                         qcc == 0, qcc == 11,
                         inc=(s_t if qcc == 11 else None), wait=w)
                tv = s_t.v

                def fqe(sc, tv=tv, th=th, bank=bank, h=h):
                    sc.wait_ge(s_t.h, tv)
                    sc.activation(sc2[:, QHEAD_O + th * TH: QHEAD_O + th * TH + TH],
                                  ps[bank][:, :], AF.Identity,
                                  bias=bias["buq"][:, h:h + 1], scale=1.0)
                    sc.activation(sc2[64:128, QEFF_O + th * TH:
                                      QEFF_O + th * TH + TH],
                                  ps[bank][64:128, :], AF.Identity,
                                  bias=bias["buq"][64:128, h:h + 1], scale=1.0)
                sqv = s_run(fqe, inc=True)
                p.guard(bank, s_s, sqv)
            wuqh_cons[slot] = s_t.v
            v_qhead = s_s.v
            if h + 2 < NH:
                wuqh_vals[slot] = load_wuqh(h + 2)
            def fqz(v, h=h):
                v.memset(sc2[ROW_B:64, QEFF_O:QEFF_O + T], 0.0)
                v.drain()
            v_qz = v_run(fqz, inc=True)
            for th in range(2):
                p.mm(2, ps[2][0:HALF, :], wqr[:, 0:HALF],
                     sc2[:, QHEAD_O + th * TH: QHEAD_O + th * TH + TH],
                     True, True, wait=[(s_s, v_qhead)])
                p.mm(3, ps[3][0:HALF, :], wqr[:, HALF:RD],
                     sc2[:, QHEAD_O + th * TH: QHEAD_O + th * TH + TH],
                     True, True, inc=s_t)
                tv = s_t.v

                def frope(v, tv=tv, th=th):
                    v.wait_ge(s_t.h, tv)
                    v.tensor_mul(A0[0:HALF, 0:TH], ps[2][0:HALF, :],
                                 cosq[:, th * TH:(th + 1) * TH])
                    v.tensor_mul(A1[0:HALF, 0:TH], ps[3][0:HALF, :],
                                 sinq[:, th * TH:(th + 1) * TH])
                    v.drain()
                    v.tensor_sub(sc2[0:HALF, QEFF_O + th * TH:
                                     QEFF_O + th * TH + TH],
                                 A0[0:HALF, 0:TH], A1[0:HALF, 0:TH])
                    v.drain()
                vr = v_run(frope, inc=True)
                p.guard(2, s_v, vr)
                p.guard(3, s_v, vr)
            v_qrope = s_v.v

            def fqsq(sc, v_qrope=v_qrope):
                sc.wait_ge(s_v.h, v_qrope)
                sc.activation(sc2[:, SQQ_O:SQQ_O + T], sc2[:, QEFF_O:QEFF_O + T],
                              AF.Square)
            sqsq = s_run(fqsq, inc=True)
            for th in range(2):
                p.mm(4 + th, ps[4 + th][0:1, :], ones_bf[:, :],
                     sc2[:, SQQ_O + th * TH: SQQ_O + th * TH + TH], True, True,
                     inc=(s_t if th == 1 else None),
                     wait=([(s_s, sqsq)] if th == 0 else None))
            tqn = s_t.v

            def fqn(sc, tqn=tqn):
                sc.wait_ge(s_t.h, tqn)
                sc.activation(row0[0:1, 0:TH], ps[4][0:1, :], AF.Sqrt)
                sc.activation(row0[0:1, TH:T], ps[5][0:1, :], AF.Sqrt)
            sqn = s_run(fqn, inc=True)
            p.guard(4, s_s, sqn)
            p.guard(5, s_s, sqn)
            kops = []
            for kc8 in range(8):
                if kc8 < 2:
                    slot_free = ("t", tqn)
                else:
                    slot_free = ("v", kops[kc8 - 2])

                def fks(sc, kc8=kc8, kb=kb, kbv=kbv, slot=slot,
                        slot_free=slot_free):
                    if kc8 == 0:
                        sc.wait_ge(s_kb[slot].h, kbv)
                    if slot_free[0] == "t":
                        sc.wait_ge(s_t.h, slot_free[1])
                    else:
                        sc.wait_ge(s_v.h, slot_free[1])
                    sc.activation(sc2[:, SQQ_O + (kc8 % 2) * TH:
                                      SQQ_O + (kc8 % 2) * TH + TH],
                                  kb[:, kc8 * TH:(kc8 + 1) * TH], AF.Square)
                sks = s_run(fks, inc=True)

                def fkr(v, sks=sks, kc8=kc8):
                    v.wait_ge(s_s.h, sks)
                    v.reduce_max(small[:, 1:2],
                                 sc2[:, SQQ_O + (kc8 % 2) * TH:
                                     SQQ_O + (kc8 % 2) * TH + TH], axis=AX)
                    v.drain()
                    if kc8 == 0:
                        v.tensor_copy(small[:, 0:1], small[:, 1:2])
                    else:
                        v.tensor_max(small[:, 0:1], small[:, 0:1], small[:, 1:2])
                    v.drain()
                kops.append(v_run(fkr, inc=True))
            v_kn = kops[-1]
            def fknf(v, v_kn=v_kn):
                v.tensor_copy(small[:, 2:3], small[:, 0:1])
                v.drain()
            vknf = v_run(fknf, inc=True)
            p.mm(6, ps[6][0:1, 0:1], ones_f[:, :], small[:, 2:3], True, True,
                 inc=s_t, wait=[(s_v, vknf)])
            tkm = s_t.v

            def fkm(sc, tkm=tkm):
                sc.wait_ge(s_t.h, tkm)
                sc.activation(small[0:1, 3:4], ps[6][0:1, 0:1], AF.Sqrt)
            skm = s_run(fkm, inc=True)
            p.guard(6, s_s, skm)
            p.mm(7, ps[7][ROW_B:ROW_B + 1, :], small[0:1, 3:4],
                 row0[0:1, 0:TH], True, True, inc=s_t,
                 wait=[(s_s, skm), (s_v, v_qz)])
            tbd = s_t.v

            def fbd(sc, tbd=tbd):
                sc.wait_ge(s_t.h, tbd)
                sc.activation(sc2[ROW_B:ROW_B + 1, QEFF_O:QEFF_O + TH],
                              ps[7][ROW_B:ROW_B + 1, :], AF.Copy)
            sbd1 = s_run(fbd, inc=True)
            p.guard(7, s_s, sbd1)
            p.mm(7, ps[7][ROW_B:ROW_B + 1, :], small[0:1, 3:4],
                 row0[0:1, TH:T], True, True, inc=s_t)
            tbd2 = s_t.v

            def fbd2(sc, tbd2=tbd2):
                sc.wait_ge(s_t.h, tbd2)
                sc.activation(sc2[ROW_B:ROW_B + 1, QEFF_O + TH:QEFF_O + T],
                              ps[7][ROW_B:ROW_B + 1, :], AF.Copy)
            sbd2 = s_run(fbd2, inc=True)
            p.guard(7, s_s, sbd2)
            v_qeff_ready = s_s.v

            for th in range(2):
                ctx_bank = 3
                dn_bank = 6
                exp_cons = {}
                for kc in range(32):
                    sb = kc % 3
                    w = []
                    if kc == 0:
                        w = [(s_s, v_qeff_ready), (s_kb[slot], kbv)]
                    p.mm(sb, ps[sb][:, :], kb[:, kc * 128:(kc + 1) * 128],
                         sc2[:, QEFF_O + th * TH: QEFF_O + th * TH + TH],
                         True, True, inc=s_t, wait=w)
                    t_sc = s_t.v

                    def fexp(sc, t_sc=t_sc, kc=kc, sb=sb, exw=exp_cons.get(kc - 4)):
                        sc.wait_ge(s_t.h, t_sc)
                        if exw is not None:
                            sc.wait_ge(s_t.h, exw)
                        sc.activation(sc2[:, EB_O + (kc % 4) * TH:
                                          EB_O + (kc % 4) * TH + TH],
                                      ps[sb][:, :], AF.Exp, scale=SCALE)
                    sev = s_run(fexp, inc=True)
                    p.guard(sb, s_s, sev)
                    w2_ = [(s_s, sev)]
                    if kc == 0:
                        w2_.append((s_vb[slot], vbv))
                        if p.bank_guard[ctx_bank] is not None:
                            w2_.append(p.bank_guard[ctx_bank])
                            p.bank_guard[ctx_bank] = None
                        if p.bank_guard[dn_bank] is not None:
                            w2_.append(p.bank_guard[dn_bank])
                            p.bank_guard[dn_bank] = None
                    p.mm(None, ps[ctx_bank][:, :],
                         vbuf[:, kc * 128:(kc + 1) * 128],
                         sc2[:, EB_O + (kc % 4) * TH: EB_O + (kc % 4) * TH + TH],
                         kc == 0, kc == 31, wait=w2_)
                    p.mm(None, ps[dn_bank][0:1, :], ones_bf[:, :],
                         sc2[:, EB_O + (kc % 4) * TH: EB_O + (kc % 4) * TH + TH],
                         kc == 0, kc == 31,
                         inc=s_t)
                    exp_cons[kc] = s_t.v
                t_ctx = s_t.v
                def frc(v, t_ctx=t_ctx):
                    v.wait_ge(s_t.h, t_ctx)
                    v.reciprocal(row0[0:1, 2 * TH:3 * TH], ps[dn_bank][0:1, :])
                    v.drain()
                vrc = v_run(frc, inc=True)
                p.guard(dn_bank, s_v, vrc)
                p.mm(5, ps[5][:, :], onesrow[0:1, :], row0[0:1, 2 * TH:3 * TH],
                     True, True, inc=s_t, wait=[(s_v, vrc)])
                trb = s_t.v

                def fctx(v, trb=trb, h=h, th=th, ctx_bank=ctx_bank):
                    v.wait_ge(s_t.h, trb)
                    v.tensor_copy(A0[:, 0:TH], ps[5][:, :])
                    v.drain()
                    v.tensor_mul(A1[:, 0:TH], ps[ctx_bank][:, :], A0[:, 0:TH])
                    v.drain()
                    v.tensor_scalar_add(wa[:, CTX_O + h * T + th * TH:
                                           CTX_O + h * T + th * TH + TH],
                                        A1[:, 0:TH], bias["buv"][:, h:h + 1])
                    v.drain()
                vctx = v_run(fctx, inc=True)
                p.guard(5, s_v, vctx)
                p.guard(ctx_bank, s_v, vctx)
            kb_cons[slot] = s_t.v
            vb_cons[slot] = s_t.v
            if h + 2 < NH:
                kb_vals[slot] = load_kb(h + 2)
                vb_vals[slot] = load_vb(h + 2)
            if h == NH - 2:
                load_wo(0)
            if h == NH - 1:
                load_wo(1)
        v_ctx_all = s_v.v

        for piece in range(16):
            slot = piece % 2
            for th in range(2):
                bank = th
                for hcb in range(HC):
                    w = None
                    if hcb == 0:
                        w = [(s_w2[slot], wo_vals[piece])]
                        if piece == 0 and th == 0:
                            w.append((s_v, v_ctx_all))
                    p.mm(bank if hcb == 0 else None, ps[bank][:, :],
                         wa[:, WO_O[slot] + hcb * 128: WO_O[slot] + hcb * 128 + 128],
                         wa[:, CTX_O + hcb * T + th * TH:
                            CTX_O + hcb * T + th * TH + TH],
                         hcb == 0, hcb == HC - 1,
                         inc=(s_t if hcb == HC - 1 else None), wait=w)
                tv = s_t.v
                emit_acc(bank, piece * T + th * TH, tv,
                         bias_col=bias["bo"][:, piece:piece + 1])
            wo_cons[slot] = s_t.v
            if piece + 2 < 16:
                load_wo(piece + 2)
        return s_v.v

    if _STAGE >= 1:
        emit_ffn()
    else:
        def fwait(sy):
            sy.wait_ge(s_in.h, v_pre_in)
        on("sync", fwait)
    p.flush_block()
    if _STAGE >= 2:
        emit_attention()
        p.flush_block()
    if _STAGE >= 3:
        emit_ffn()
        p.flush_block()

    fin_tgt = s_out.bump()

    def fst(sy, fin_tgt=fin_tgt):
        sy.dma_start(out=y_out[:, :].rearrange("(hc p) t -> p hc t", p=128),
                     in_=x_fm[:, :].rearrange("p (hc t) -> p hc t", t=T))\
            .then_inc(s_out.h, 16)
        sy.wait_ge(s_out.h, fin_tgt)
    on("sync", fst)
    p.flush_block()

    ctx.close()
    return nc



def kernel(**inputs):
    shared = _prep_shared(inputs)
    nc = build_program()
    in_maps = []
    for c in range(NCORE):
        m = dict(shared)
        m.update(_prep_core(inputs, c))
        in_maps.append(m)
    res = run_bass_kernel_spmd(nc, in_maps, core_ids=list(range(NCORE)))
    out = np.empty((B, S, HID), np.float32)
    for c in range(NCORE):
        b, part = c // 4, c % 4
        y = res.results[c]["y"]
        out[b, part * T:(part + 1) * T, :] = y.T
    return out
